# version 07
# baseline (speedup 1.0000x reference)
import contextlib
from collections import defaultdict

import numpy as np
import ml_dtypes
import concourse.bass as bass
import concourse.mybir as mybir
from concourse.bass_utils import run_bass_kernel_spmd

F32 = mybir.dt.float32
BF16 = mybir.dt.bfloat16
I32 = mybir.dt.int32
AF = mybir.ActivationFunctionType
ALU = mybir.AluOpType
AXX = mybir.AxisListType.X
BF = ml_dtypes.bfloat16

B, S, D, HD = 4, 2048, 4096, 128
NH, NKV, LAT = 32, 8, 1024
DC = D // 128
LC = LAT // 128
NT = S // 128
NO = 1024
NR, NE = 15, 8
SCALE = 1.0 / float(np.sqrt(HD))
SHIFT = -12.0
NEG = -3.0e38
EPS = 1e-5
THETA = 10000.0
OWN_TILES = [[0, 1, 2, 3, 12, 13, 14, 15], [4, 5, 6, 7, 8, 9, 10, 11]]
OWN_BASE = [[0, 12], [4, 8]]
SLOT_KT = [8, 16]

ENGS = ("sync", "gpsimd", "tensor", "vector", "scalar")

UPTO = "full"
SUB2A = "all"
DEV_EXPOSE = False
LAST_RES = None


class KB:
    def __init__(self):
        self.ops = {e: [] for e in ENGS}
        self.cnt = defaultdict(int)
        self.sem = {}
        self.last_wait = {}

    def val(self, name):
        return self.cnt[name]

    def op(self, eng, fn, waits=(), inc=None, tag=""):
        w = tuple((s, v) for (s, v) in waits if v > 0)
        v = None
        if inc is not None:
            self.cnt[inc[0]] += inc[1]
            v = self.cnt[inc[0]]
        self.ops[eng].append({"fn": fn, "waits": w, "inc": inc, "tag": tag})
        return v

    def barrier(self, tag="bar"):
        snap = [(s, v) for s, v in self.cnt.items() if v > 0]
        for e in ENGS:
            self.op(e, None, waits=snap, tag=tag)

    def check(self):
        ptr = {e: 0 for e in ENGS}
        semv = defaultdict(int)
        total = sum(len(v) for v in self.ops.values())
        done = 0
        progress = True
        while progress:
            progress = False
            for e in ENGS:
                while ptr[e] < len(self.ops[e]):
                    o = self.ops[e][ptr[e]]
                    if all(semv[s] >= v for s, v in o["waits"]):
                        if o["inc"] is not None:
                            semv[o["inc"][0]] += o["inc"][1]
                        ptr[e] += 1
                        done += 1
                        progress = True
                    else:
                        break
        if done < total:
            stuck = {}
            for e in ENGS:
                if ptr[e] < len(self.ops[e]):
                    o = self.ops[e][ptr[e]]
                    blocked = [(s, v, semv[s]) for s, v in o["waits"] if semv[s] < v]
                    stuck[e] = (ptr[e], o["tag"], blocked)
            raise RuntimeError(f"DEADLOCK: {stuck}")
        for e in ENGS:
            for o in self.ops[e]:
                for s, v in o["waits"]:
                    assert v <= self.cnt[s], \
                        f"wait {s}>={v} but total {self.cnt[s]} ({o['tag']})"

    def emit(self, block):
        def mk(ename):
            def body(eng):
                for o in self.ops[ename]:
                    for s, v in o["waits"]:
                        k = (ename, s)
                        if self.last_wait.get(k, 0) >= v:
                            continue
                        self.last_wait[k] = v
                        eng.wait_ge(self.sem[s], v)
                    if o["fn"] is not None:
                        r = o["fn"](eng)
                        if o["inc"] is not None:
                            s, by = o["inc"]
                            r.then_inc(self.sem[s], by)
            return body
        for ename, meth in (("sync", block.sync), ("gpsimd", block.gpsimd),
                            ("tensor", block.tensor), ("vector", block.vector),
                            ("scalar", block.scalar)):
            if self.ops[ename]:
                meth(mk(ename))


def _rope_tables():
    inv = THETA ** (-np.arange(0, HD, 2, dtype=np.float64) / HD)
    t = np.arange(S, dtype=np.float64)
    fr = np.outer(t, inv)
    emb = np.concatenate([fr, fr], axis=1)
    cos = np.cos(emb).T.astype(np.float32)
    sin = np.sin(emb).T.astype(np.float32)
    ssgn = sin.copy()
    ssgn[0:64, :] = -ssgn[0:64, :]
    return cos, ssgn


def _host_consts(parity):
    cos, ssgn = _rope_tables()
    own = np.concatenate([np.arange(128 * t, 128 * t + 128) for t in OWN_TILES[parity]])
    cosO = np.ascontiguousarray(cos[:, own])
    sinO = np.ascontiguousarray(ssgn[:, own])
    bias = np.full((128, 24), NEG, dtype=np.float32)
    masks = np.ones((128, 16, 512), dtype=np.float32)
    for cc in range(2):
        base = OWN_BASE[parity][cc]
        for kt in range(SLOT_KT[cc]):
            if kt <= base + 3:
                bias[:, 8 * cc + kt] = SHIFT
        for kt in (range(8) if cc == 0 else range(8, 16)):
            if base <= kt <= base + 3:
                j = kt - base
                p = np.arange(128)[:, None]
                c = np.arange(512)[None, :]
                masks[:, kt, :] = (c >= 128 * j + p).astype(np.float32)
    masks = np.ascontiguousarray(masks.reshape(128, 16 * 512)).astype(BF)
    return own, cosO, sinO, cos, ssgn, bias, masks


def _p8pad():
    p = np.zeros((128, 128), dtype=np.float32)
    for e in range(NE):
        p[e, e * 16:(e + 1) * 16] = 2.0
    return p


def _wbands(w, nch, nj, ncol):
    return np.ascontiguousarray(
        w.reshape(nch, 128, nj, ncol).transpose(2, 1, 0, 3).reshape(nj * 128, nch * ncol))


def _xmajor(xt, ncols):
    return np.ascontiguousarray(
        xt.reshape(32, 128, ncols).transpose(1, 0, 2).reshape(128, 32 * ncols))


def build_program(upto=UPTO, expose=DEV_EXPOSE, sub2a=None):
    if sub2a is None:
        sub2a = SUB2A
    nc = bass.Bass("TRN2", target_bir_lowering=False, debug=False)
    kb = KB()
    SEM = kb.sem
    phases = ["ln", "ffn", "q", "kv", "attn", "wo", "lat", "wlo", "full"]
    plim = phases.index(upto)

    def enabled(ph):
        return phases.index(ph) <= plim

    def din(name, shape, dt=F32):
        return nc.dram_tensor(name, shape, dt, kind="ExternalInput").ap()

    def dscr(name, shape, dt=BF16):
        kind = "ExternalOutput" if expose else "Internal"
        return nc.dram_tensor(name, shape, dt, kind=kind).ap()

    xoH = din("xoH", [128, DC * NO])
    xbH = din("xbH", [128, DC * S])
    wqB = din("wqB", [32 * 128, DC * 128])
    wkB = din("wkB", [8 * 128, DC * 128])
    wvB = din("wvB", [2 * 128, DC * 512])
    wlqB = din("wlqB", [8 * 128, DC * 128])
    wlkB = din("wlkB", [32 * 128, LC * 128])
    wlvB = din("wlvB", [8 * 128, DC * 512])
    woB = din("woB", [32 * 128, DC * 128])
    wloB = din("wloB", [32 * 128, DC * 128])
    bWB = din("bWB", [32 * 128, DC * 128])
    rWB = din("rWB", [128, DC * NR])
    ATB = din("ATB", [128, DC * 128])
    BTB = din("BTB", [128, D])
    bbI = din("bbI", [128, 32])
    rbI = din("rbI", [NR, 1])
    emapI = din("emapI", [NR, 1], I32)
    iotaI = din("iotaI", [NR, NE])
    p8I = din("p8I", [128, 128])
    onesI = din("onesI", [128, 128], BF16)
    eyeI = din("eyeI", [128, 128])
    cosO_I = din("cosO", [128, NO], BF16)
    sinO_I = din("sinO", [128, NO], BF16)
    cosF_I = din("cosF", [128, S], BF16)
    sinF_I = din("sinF", [128, S], BF16)
    biasI = din("biasI", [128, 24])
    maskI = din("maskI", [128, 16 * 512], BF16)

    outT = nc.dram_tensor("outT", [D, NO], F32, kind="ExternalOutput").ap()
    if expose:
        xn_dbg = dscr("xn_dbg", [128, DC * 1024])
        hw_dbg = dscr("hw_dbg", [128, NO])
        lg_dbg = dscr("lg_dbg", [NR, NO], F32)
        mp_dbg = dscr("mp_dbg", [NR, NO], F32)
        at_dbg = dscr("at_dbg", [128, DC * 1024])
        lat_dbg = dscr("lat_dbg", [128, DC * 1024])
    q_scr = dscr("q_scr", [D, NO])
    k_scr = dscr("k_scr", [LAT, S])
    lk_scr = dscr("lk_scr", [D, S])
    v_scr = dscr("v_scr", [128, NKV * NT * 128])
    lv_scr = dscr("lv_scr", [128, NH * NT * 128])

    big = contextlib.ExitStack()
    with big:
        for n in ["cst", "sW", "sX", "sSt", "sAc", "sPE", "sDV", "sAC",
                  "aS", "aP", "aM", "aC", "aE"]:
            SEM[n] = big.enter_context(nc.semaphore(n))

        onesb = big.enter_context(nc.sbuf_tensor("onesb", [128, 128], BF16))
        eyeS = big.enter_context(nc.sbuf_tensor("eyeS", [128, 128], F32))
        biasT = big.enter_context(nc.sbuf_tensor("biasT", [128, 24], F32))
        bbS = big.enter_context(nc.sbuf_tensor("bbS", [128, 32], F32))
        epsT = big.enter_context(nc.sbuf_tensor("epsT", [128, 1], F32))
        hwT = big.enter_context(nc.sbuf_tensor("hwT", [128, NO], BF16))
        bigA = big.enter_context(nc.sbuf_tensor("bigA", [128, DC * 1024], BF16))

        blk = big.enter_context(nc.Block())

        _ctr = [0]

        def nm():
            _ctr[0] += 1
            return f"t{_ctr[0]}"

        def dma(eng, dst, src, sem, waits=(), tag="", accum=False):
            if accum:
                fn = lambda e, d=dst, s=src: e.dma_start(out=d, in_=s, accum_op=ALU.add)
            else:
                fn = lambda e, d=dst, s=src: e.dma_start(out=d, in_=s)
            return kb.op(eng, fn, waits=waits, inc=(sem, 16), tag=tag)

        for dst, src in ((onesb[:, :], onesI), (eyeS[:, :], eyeI), (biasT[:, :], biasI),
                         (bbS[:, :], bbI)):
            dma("sync", dst, src, "cst", tag="cst")
        kb.op("vector", lambda e: e.memset(epsT[:, :], EPS), tag="eps")
        kb.op("vector", lambda e: e.tensor_copy(epsT[0:1, 0:1], epsT[0:1, 0:1]),
              inc=("sDV", 1), tag="eps.ms")
        kb.barrier("b.cst")

        def alloc_ln(sc):
            CW = 256
            b = {"CW": CW}
            b["xbf"] = [sc.enter_context(nc.sbuf_tensor(nm(), [128, DC * CW], BF16))
                        for _ in range(2)]
            b["sqr"] = [sc.enter_context(nc.sbuf_tensor(nm(), [128, CW], BF16))
                        for _ in range(4)]
            for k in ["negm", "qm", "mm2", "varb", "sdb"]:
                b[k] = sc.enter_context(nc.sbuf_tensor(nm(), [128, CW], F32))
            b["rstd"] = [sc.enter_context(nc.sbuf_tensor(nm(), [128, CW], F32))
                         for _ in range(2)]
            b["bb"] = [sc.enter_context(nc.sbuf_tensor(nm(), [128, CW], F32))
                       for _ in range(2)]
            b["tmp"] = sc.enter_context(nc.sbuf_tensor(nm(), [128, CW], F32))
            b["psS"] = [sc.enter_context(nc.psum_tensor(nm(), [128, CW], F32))]
            b["psQ"] = [sc.enter_context(nc.psum_tensor(nm(), [128, CW], F32))]
            return b

        def emit_ln(b, src_cols, ntc, dst, dst_off):
            CW = b["CW"]
            applies = {}
            stats_done = {}
            sqrts = {}
            for i in range(ntc):
                xb2 = b["xbf"][i % 2]
                psS, psQ = b["psS"][0], b["psQ"][0]
                rstd, bbb = b["rstd"][i % 2], b["bb"][i % 2]
                vW = dma("gpsimd", xb2[:, :].rearrange("p (c n) -> p c n", n=CW),
                         src_cols(i), "sW",
                         waits=[("sDV", applies.get(i - 2, 0))], tag="ln.x")
                pe0 = kb.val("sPE")
                for c in range(DC):
                    vA = kb.op("scalar",
                               lambda e, o=b["sqr"][c % 4][:, :],
                               i_=xb2[:, CW * c:CW * c + CW]:
                               e.activation(out=o, in_=i_, func=AF.Square),
                               waits=[("sW", vW), ("sPE", pe0 + (c - 3))],
                               inc=("sAC", 1), tag="ln.sq")
                    kb.op("tensor",
                          lambda e, i_=xb2[:, CW * c:CW * c + CW],
                          st=(c == 0), sp=(c == DC - 1), ps=psS:
                          e.matmul(ps[:, :], onesb[:, :], i_, start=st, stop=sp),
                          waits=([("sW", vW), ("sDV", stats_done.get(i - 1, 0))]
                                 if c == 0 else []),
                          tag="ln.s")
                    kb.op("tensor",
                          lambda e, i_=b["sqr"][c % 4][:, :],
                          st=(c == 0), sp=(c == DC - 1), ps=psQ:
                          e.matmul(ps[:, :], onesb[:, :], i_, start=st, stop=sp),
                          waits=[("sAC", vA)], inc=("sPE", 1), tag="ln.q")
                pe_done = kb.val("sPE")

                def stats(e, b=b, psS=psS, psQ=psQ):
                    e.tensor_scalar_mul(b["negm"][:, :], psS[:, :], -1.0 / D)
                    e.tensor_scalar_mul(b["qm"][:, :], psQ[:, :], 1.0 / D)
                    e.drain()
                    e.tensor_mul(b["mm2"][:, :], b["negm"][:, :], b["negm"][:, :])
                    e.drain()
                    return e.tensor_sub(b["varb"][:, :], b["qm"][:, :], b["mm2"][:, :])
                vD = kb.op("vector", stats,
                           waits=[("sPE", pe_done), ("sAC", sqrts.get(i - 1, 0))],
                           inc=("sDV", 1), tag="ln.st")
                stats_done[i] = vD
                vA2 = kb.op("scalar",
                            lambda e, b=b: e.activation(
                                out=b["sdb"][:, :], in_=b["varb"][:, :],
                                func=AF.Sqrt, bias=epsT[:, 0:1]),
                            waits=[("sDV", vD)], inc=("sAC", 1), tag="ln.sqrt")
                sqrts[i] = vA2

                def fin(e, b=b, rstd=rstd, bbb=bbb):
                    e.reciprocal(rstd[:, :], b["sdb"][:, :])
                    e.drain()
                    return e.tensor_mul(bbb[:, :], b["negm"][:, :], rstd[:, :])
                vD2 = kb.op("vector", fin, waits=[("sAC", vA2)],
                            inc=("sDV", 1), tag="ln.fin")

                def apply(e, b=b, ii=i, xb2=xb2, rstd=rstd, bbb=bbb):
                    for c in range(DC):
                        e.tensor_mul(b["tmp"][:, :], xb2[:, CW * c:CW * c + CW],
                                     rstd[:, :])
                        e.drain()
                        e.tensor_add(dst[:, 1024 * c + dst_off + CW * ii:
                                         1024 * c + dst_off + CW * ii + CW],
                                     b["tmp"][:, :], bbb[:, :])
                        e.drain()
                    return e.tensor_copy(b["tmp"][0:1, 0:1], b["tmp"][0:1, 0:1])
                applies[i] = kb.op("vector", apply, waits=[("sDV", vD2)],
                                   inc=("sDV", 1), tag="ln.ap")

        def emit_proj_fm(wsrc, nch, njs, rhs, rhs_stride, ntc, bufs, rope, cos_sin,
                         cos_off, dst_dram=None, dst_sbuf=None, tag="pj"):
            wb, stg, t1, t2, psP = bufs
            PEB = kb.val("sPE")
            DVB = kb.val("sDV")
            STB = kb.val("sSt")
            for ji in range(njs):
                vW = dma("gpsimd", wb[ji % 2][:, 0:nch * 128],
                         wsrc[128 * ji:128 * ji + 128, :],
                         "sW", waits=[("sPE", PEB + max(0, (ji - 1) * ntc))],
                         tag=tag + ".w")
                for tc in range(ntc):
                    g = ji * ntc + tc
                    pb = g % 3
                    vM = None
                    for c in range(nch):
                        st, sp = (c == 0), (c == nch - 1)
                        w = [("sW", vW), ("sDV", DVB + max(0, g - 2))] if st else []
                        fn = (lambda e, ps=psP[pb][:, :],
                              l=wb[ji % 2][:, 128 * c:128 * c + 128],
                              r=rhs[:, rhs_stride * c + 512 * tc:
                                    rhs_stride * c + 512 * tc + 512],
                              s1=st, s2=sp: e.matmul(ps, l, r, start=s1, stop=s2))
                        if sp:
                            vM = kb.op("tensor", fn, waits=w, inc=("sPE", 1),
                                       tag=f"{tag}.mm")
                        else:
                            kb.op("tensor", fn, waits=w, tag=f"{tag}.mm")
                    si = g % 4
                    stw = [("sPE", vM)]
                    if dst_dram is not None:
                        stw.append(("sSt", STB + 16 * max(0, g - 3)))
                    if rope:
                        cosb, sinb = cos_sin
                        co = cos_off + 512 * tc

                        def evac(e, ps=psP[pb], s=stg[si], co=co):
                            e.tensor_mul(t1[:, :], ps[:, :], cosb[:, co:co + 512])
                            e.tensor_mul(t2[0:64, :], ps[64:128, :],
                                         sinb[0:64, co:co + 512])
                            e.tensor_mul(t2[64:128, :], ps[0:64, :],
                                         sinb[64:128, co:co + 512])
                            e.drain()
                            return e.tensor_add(s[:, :], t1[:, :], t2[:, :])
                    elif dst_dram is not None:
                        def evac(e, ps=psP[pb], s=stg[si]):
                            return e.tensor_copy(s[:, :], ps[:, :])
                    else:
                        def evac(e, ps=psP[pb], jj=ji, tcc=tc):
                            return e.tensor_copy(
                                dst_sbuf[:, 1024 * jj + 512 * tcc:
                                         1024 * jj + 512 * tcc + 512], ps[:, :])
                    vE = kb.op("vector", evac, waits=stw, inc=("sDV", 1),
                               tag=f"{tag}.ev")
                    if dst_dram is not None:
                        dma("sync",
                            dst_dram[128 * ji:128 * ji + 128, 512 * tc:512 * tc + 512],
                            stg[si][:, :], "sSt", waits=[("sDV", vE)], tag=tag + ".st")

        with contextlib.ExitStack() as sc:
            lb = alloc_ln(sc)
            xo3 = xoH.rearrange("p (c n) -> p c n", n=NO)
            emit_ln(lb, lambda tc: xo3[:, :, 256 * tc:256 * tc + 256], NO // 256,
                    bigA, 0)
            kb.barrier("b.ln")
            if expose:
                for i in range(8):
                    dma("sync", xn_dbg[:, 4096 * i:4096 * i + 4096],
                        bigA[:, 4096 * i:4096 * i + 4096], "sSt", tag="dbg.xn")
                kb.barrier("b.dbg1")
        if not enabled("ffn"):
            kb.check(); kb.emit(blk); return nc, kb

        with contextlib.ExitStack() as sc:
            E = lambda n, sh, dt=F32: sc.enter_context(nc.sbuf_tensor(n, sh, dt))
            rWTb = E("rWTb", [128, DC * NR], BF16)
            ATb = E("ATb", [128, DC * 128], BF16)
            mpS = E("mpS", [128, NO])
            wsb = E("wsb", [128, 512])
            wexpS = E("wexpS", [128, 512])
            EmapP = E("EmapP", [128, NE])
            p8S = E("p8S", [128, 128])
            iotaS = E("iotaS", [NR, NE])
            emapS = E("emapS", [NR, 1], I32)
            emapF = E("emapF", [NR, 1])
            rbS = E("rbS", [NR, 1])
            lgf = E("lgf", [NR, NO])
            ltok = E("ltok", [128, NR])
            eo = E("eo", [128, NR])
            nmax = E("nmax", [128, 1])
            sume = E("sume", [128, 1])
            recp = E("recp", [128, 1])
            beat = E("beat", [128, NR])
            eqc = E("eqc", [128, NR])
            tot = E("tot", [128, NR])
            mps = E("mps", [128, NR])
            junk = E("junk", [128, NR])
            psR = sc.enter_context(nc.psum_tensor("psR", [128, 512], F32))
            psT = sc.enter_context(nc.psum_tensor("psT", [128, 128], F32))
            psW = sc.enter_context(nc.psum_tensor("psW", [128, 512], F32))
            psH = sc.enter_context(nc.psum_tensor("psH", [128, 512], F32))

            vW1 = dma("gpsimd", rWTb[:, :], rWB, "sW", tag="rt.w")
            vW2 = dma("gpsimd", ATb[:, :], ATB, "sW", tag="rt.at")
            for dst, src in ((p8S[:, :], p8I), (iotaS[:, :], iotaI),
                             (emapS[:, :], emapI), (rbS[:, :], rbI)):
                dma("sync", dst, src, "cst", tag="rt.c")
            CC = kb.val("cst")

            def prep(e):
                e.tensor_copy(emapF[:, :], emapS[:, :])
                e.memset(EmapP[:, :], 0.0)
                e.drain()
                e.tensor_single_scalar(EmapP[0:NR, :], iotaS[:, :], emapF[:, 0:1],
                                       ALU.is_equal)
                e.drain()
                return e.tensor_copy(EmapP[0:1, 0:1], EmapP[0:1, 0:1])
            vPrep = kb.op("vector", prep, waits=[("cst", CC)], inc=("sDV", 1),
                          tag="rt.prep")

            for tc in range(2):
                vP = None
                for c in range(DC):
                    st, sp = (c == 0), (c == DC - 1)
                    w = [("sW", vW1), ("sDV", kb.val("sDV"))] if st else []
                    fn = (lambda e, l=rWTb[:, NR * c:NR * c + NR],
                          r=bigA[:, 1024 * c + 512 * tc:1024 * c + 512 * tc + 512],
                          s1=st, s2=sp:
                          e.matmul(psR[0:NR, :], l, r, start=s1, stop=s2))
                    if sp:
                        vP = kb.op("tensor", fn, waits=w, inc=("sPE", 1), tag="rt.mm")
                    else:
                        kb.op("tensor", fn, waits=w, tag="rt.mm")
                kb.op("vector",
                      lambda e, d=lgf[:, 512 * tc:512 * tc + 512]:
                      e.tensor_scalar_add(d, psR[0:NR, 0:512], rbS[:, 0:1]),
                      waits=[("sPE", vP), ("cst", CC)], inc=("sDV", 1), tag="rt.lg")
            LGD = kb.val("sDV")
            if expose and sub2a in ("lg", "tk"):
                dma("sync", lg_dbg, lgf[0:NR, :], "sSt",
                    waits=[("sDV", LGD)], tag="dbg.lg")
            if sub2a == "lg":
                kb.barrier("b.rt")
                kb.check(); kb.emit(blk); return nc, kb

            for tt in range(NO // 128):
                vT = kb.op("tensor",
                           lambda e, i=lgf[0:NR, 128 * tt:128 * tt + 128]:
                           e.transpose(psT[:, 0:NR], i, eyeS[0:NR, 0:NR]),
                           waits=[("sDV", LGD), ("sDV", kb.val("sDV"))],
                           inc=("sPE", 1), tag="tk.tr")

                def tokstat(e):
                    e.tensor_copy(ltok[:, :], psT[:, 0:NR])
                    e.drain()
                    e.reduce_max(nmax[:, :], ltok[:, :], axis=AXX, negate=True)
                    e.drain()
                    return e.tensor_copy(nmax[0:1, 0:1], nmax[0:1, 0:1])
                vDm = kb.op("vector", tokstat, waits=[("sPE", vT)],
                            inc=("sDV", 1), tag="tk.st")
                vAe = kb.op("scalar",
                            lambda e: e.activation(out=eo[:, :], in_=ltok[:, :],
                                                   func=AF.Exp, bias=nmax[:, 0:1],
                                                   accum_out=sume[:, :]),
                            waits=[("sDV", vDm)], inc=("sAC", 1), tag="tk.exp")

                def select(e):
                    e.reciprocal(recp[:, :], sume[:, :])
                    e.memset(eqc[:, :], 0.0)
                    for i in range(NR):
                        e.tensor_scalar(junk[:, :], eo[:, :], eo[:, i:i + 1], None,
                                        ALU.is_gt, op1=ALU.add,
                                        accum_out=beat[:, i:i + 1])
                    for i in range(1, NR):
                        e.tensor_scalar(junk[:, 0:i], eo[:, 0:i], eo[:, i:i + 1], None,
                                        ALU.is_equal, op1=ALU.add,
                                        accum_out=eqc[:, i:i + 1])
                    e.drain()
                    e.tensor_add(tot[:, :], beat[:, :], eqc[:, :])
                    e.drain()
                    e.tensor_single_scalar(tot[:, :], tot[:, :], 7.5, ALU.is_lt)
                    e.drain()
                    e.tensor_mul(mps[:, :], tot[:, :], eo[:, :])
                    e.drain()
                    e.tensor_scalar_mul(mps[:, :], mps[:, :], recp[:, 0:1])
                    e.drain()
                    return e.tensor_copy(mps[0:1, 0:1], mps[0:1, 0:1])
                vDs = kb.op("vector", select, waits=[("sAC", vAe)],
                            inc=("sDV", 1), tag="tk.sel")
                vT2 = kb.op("tensor",
                            lambda e: e.transpose(psT[0:NR, 0:128], mps[:, :],
                                                  eyeS[:, :]),
                            waits=[("sDV", vDs)], inc=("sPE", 1), tag="tk.tr2")
                kb.op("vector",
                      lambda e, d=mpS[0:NR, 128 * tt:128 * tt + 128]:
                      e.tensor_copy(d, psT[0:NR, 0:128]),
                      waits=[("sPE", vT2)], inc=("sDV", 1), tag="tk.cp")

            if expose and sub2a == "tk":
                dma("sync", mp_dbg, mpS[0:NR, :], "sSt",
                    waits=[("sDV", kb.val("sDV"))], tag="dbg.mp")
            if sub2a == "tk":
                kb.barrier("b.rt")
                kb.check(); kb.emit(blk); return nc, kb

            for tc in range(2):
                vP1 = kb.op("tensor",
                            lambda e, r=mpS[0:NR, 512 * tc:512 * tc + 512]:
                            e.matmul(psW[0:NE, 0:512], EmapP[0:NR, 0:NE], r,
                                     start=True, stop=True),
                            waits=[("sDV", kb.val("sDV")), ("sDV", vPrep)],
                            inc=("sPE", 1), tag="lw.w")
                vD1 = kb.op("vector",
                            lambda e: e.tensor_copy(wsb[0:NE, :], psW[0:NE, 0:512]),
                            waits=[("sPE", vP1)], inc=("sDV", 1), tag="lw.cp")
                vP2 = kb.op("tensor",
                            lambda e: e.matmul(psW[:, :], p8S[0:NE, :], wsb[0:NE, :],
                                               start=True, stop=True),
                            waits=[("sDV", vD1), ("cst", CC)],
                            inc=("sPE", 1), tag="lw.ex")
                vD2 = kb.op("vector",
                            lambda e: e.tensor_copy(wexpS[:, :], psW[:, :]),
                            waits=[("sPE", vP2)], inc=("sDV", 1), tag="lw.excp")
                vPh = None
                for c in range(DC):
                    st, sp = (c == 0), (c == DC - 1)
                    w = [("sW", vW2), ("sDV", vD2)] if st else []
                    fn = (lambda e, l=ATb[:, 128 * c:128 * c + 128],
                          r=bigA[:, 1024 * c + 512 * tc:1024 * c + 512 * tc + 512],
                          s1=st, s2=sp:
                          e.matmul(psH[:, :], l, r, start=s1, stop=s2))
                    if sp:
                        vPh = kb.op("tensor", fn, waits=w, inc=("sPE", 1), tag="lw.h")
                    else:
                        kb.op("tensor", fn, waits=w, tag="lw.h")
                kb.op("vector",
                      lambda e, d=hwT[:, 512 * tc:512 * tc + 512]:
                      e.tensor_mul(d, psH[:, :], wexpS[:, :]),
                      waits=[("sPE", vPh)], inc=("sDV", 1), tag="lw.hw")
            if expose:
                dma("sync", hw_dbg[:, :], hwT[:, :], "sSt",
                    waits=[("sDV", kb.val("sDV"))], tag="dbg.hw")
            kb.barrier("b.rt")
        if not enabled("q"):
            kb.check(); kb.emit(blk); return nc, kb

        with contextlib.ExitStack() as scp:
            wb = [scp.enter_context(nc.sbuf_tensor(nm(), [128, DC * 128], BF16))
                  for _ in range(2)]
            stg = [scp.enter_context(nc.sbuf_tensor(nm(), [128, 512], BF16))
                   for _ in range(4)]
            t1 = scp.enter_context(nc.sbuf_tensor(nm(), [128, 512], F32))
            t2 = scp.enter_context(nc.sbuf_tensor(nm(), [128, 512], F32))
            psP = [scp.enter_context(nc.psum_tensor(nm(), [128, 512], F32))
                   for _ in range(3)]
            scq = contextlib.ExitStack()
            cosOb = scq.enter_context(nc.sbuf_tensor("cosOb", [128, NO], BF16))
            sinOb = scq.enter_context(nc.sbuf_tensor("sinOb", [128, NO], BF16))
            for dst, src in ((cosOb[:, :], cosO_I), (sinOb[:, :], sinO_I)):
                dma("sync", dst, src, "cst", tag="ro.c")
            kb.barrier("b.q0")

            emit_proj_fm(wqB, DC, 32, bigA, 1024, 2, (wb, stg, t1, t2, psP),
                         True, (cosOb, sinOb), 0, dst_dram=q_scr, tag="q")
            kb.barrier("b.q")
            scq.close()

            with contextlib.ExitStack() as sc:
                BTIb = sc.enter_context(nc.sbuf_tensor("BTIb", [128, D], BF16))
                xot = [sc.enter_context(nc.sbuf_tensor(nm(), [128, 512], F32))
                       for _ in range(4)]
                ost = [sc.enter_context(nc.sbuf_tensor(nm(), [128, 512], F32))
                       for _ in range(4)]
                vWB = dma("gpsimd", BTIb[:, :], BTB, "sW", tag="fo.bt")
                PEB = kb.val("sPE")
                DVB = kb.val("sDV")
                STB = kb.val("sSt")
                for j in range(DC):
                    vW = dma("gpsimd", wb[j % 2][:, :],
                             bWB[128 * j:128 * j + 128, :],
                             "sW", waits=[("sPE", PEB + max(0, (j - 1) * 2))],
                             tag="fo.w")
                    for tc in range(2):
                        g = j * 2 + tc
                        pb = g % 3
                        vX = dma("sync", xot[g % 4][:, :],
                                 xoH[:, j * NO + 512 * tc:j * NO + 512 * tc + 512],
                                 "sX", waits=[("sDV", DVB + max(0, g - 3))],
                                 tag="fo.x")
                        for c in range(DC):
                            st = (c == 0)
                            w = ([("sW", vW), ("sDV", DVB + max(0, g - 2))]
                                 if st else [])
                            fn = (lambda e, l=wb[j % 2][:, 128 * c:128 * c + 128],
                                  r=bigA[:, 1024 * c + 512 * tc:
                                         1024 * c + 512 * tc + 512],
                                  s1=st, ps=psP[pb][:, :]:
                                  e.matmul(ps, l, r, start=s1, stop=False))
                            kb.op("tensor", fn, waits=w, tag="fo.mm")
                        vM = kb.op("tensor",
                                   lambda e, l=BTIb[:, 128 * j:128 * j + 128],
                                   r=hwT[:, 512 * tc:512 * tc + 512],
                                   ps=psP[pb][:, :]:
                                   e.matmul(ps, l, r, start=False, stop=True),
                                   waits=[("sW", vWB)], inc=("sPE", 1), tag="fo.lb")
                        si = g % 4

                        def evac(e, ps=psP[pb], s=ost[si], jj=j, xo=xot[g % 4]):
                            e.tensor_scalar_add(t1[:, :], ps[:, :], bbS[:, jj:jj + 1])
                            e.drain()
                            return e.tensor_add(s[:, :], t1[:, :], xo[:, :])
                        vE = kb.op("vector", evac,
                                   waits=[("sPE", vM), ("sX", vX),
                                          ("sSt", STB + 16 * max(0, g - 3))],
                                   inc=("sDV", 1), tag="fo.ev")
                        dma("sync",
                            outT[128 * j:128 * j + 128, 512 * tc:512 * tc + 512],
                            ost[si][:, :], "sSt", waits=[("sDV", vE)], tag="fo.st")
                kb.barrier("b.fo")
            if not enabled("kv"):
                kb.check(); kb.emit(blk); return nc, kb

            with contextlib.ExitStack() as sc:
                cosFb = sc.enter_context(nc.sbuf_tensor("cosFb", [128, S], BF16))
                sinFb = sc.enter_context(nc.sbuf_tensor("sinFb", [128, S], BF16))
                for dst, src in ((cosFb[:, :], cosF_I), (sinFb[:, :], sinF_I)):
                    dma("sync", dst, src, "cst", tag="kv.c")
                xb3 = xbH.rearrange("p (c n) -> p c n", n=S)
                wvb = [sc.enter_context(nc.sbuf_tensor(nm(), [128, DC * 512], BF16))
                       for _ in range(2)]
                psV = [sc.enter_context(nc.psum_tensor(nm(), [128, 512], F32))
                       for _ in range(2)]
                vpre = {}
                vlast_pe = [0, 0]
                for half in range(2):
                    toff = 1024 * half
                    for jc in range(2):
                        wsrc0 = wvB
                        vpre[(half, jc)] = dma(
                            "gpsimd", wvb[jc][:, :],
                            wsrc0[128 * jc:128 * jc + 128, :], "sW",
                            waits=[("sPE", vlast_pe[jc])], tag="v.wpre")
                    with contextlib.ExitStack() as s2:
                        lb = alloc_ln(s2)
                        emit_ln(lb,
                                lambda tc, toff=toff:
                                xb3[:, :, toff + 256 * tc:toff + 256 * tc + 256],
                                4, bigA, 0)
                        kb.barrier(f"b.kvln{half}")
                    with contextlib.ExitStack() as s2:
                        PEB = kb.val("sPE")
                        DVB = kb.val("sDV")
                        STB = kb.val("sSt")
                        gall = [0]
                        for wsrc, njc, dscr_t, nheads in ((wvB, 2, v_scr, NKV),
                                                          (wlvB, 8, lv_scr, NH)):
                            for jc in range(njc):
                                if wsrc is wvB:
                                    vW = vpre[(half, jc)]
                                else:
                                    vW = dma("gpsimd", wvb[jc % 2][:, :],
                                             wsrc[128 * jc:128 * jc + 128, :],
                                             "sW",
                                             waits=[("sPE",
                                                     PEB + max(0, gall[0] - 8))],
                                             tag="v.w")
                                if wsrc is wlvB and jc >= 6:
                                    vlast_pe[jc % 2] = PEB + 8 * (jc + 1)
                                for tt in range(8):
                                    g = gall[0]
                                    gall[0] += 1
                                    pb = g % 2
                                    vM = None
                                    for c in range(DC):
                                        st, sp = (c == 0), (c == DC - 1)
                                        w = ([("sW", vW),
                                              ("sDV", DVB + max(0, g - 1))]
                                             if st else [])
                                        fn = (lambda e,
                                              l=bigA[:, 1024 * c + 128 * tt:
                                                     1024 * c + 128 * tt + 128],
                                              r=wvb[jc % 2][:, 512 * c:512 * c + 512],
                                              s1=st, s2=sp, ps=psV[pb][:, :]:
                                              e.matmul(ps, l, r, start=s1, stop=s2))
                                        if sp:
                                            vM = kb.op("tensor", fn, waits=w,
                                                       inc=("sPE", 1), tag="v.mm")
                                        else:
                                            kb.op("tensor", fn, waits=w, tag="v.mm")
                                    si = g % 4
                                    vE = kb.op(
                                        "vector",
                                        lambda e, ps=psV[pb], s=stg[si]:
                                        e.tensor_copy(s[:, 0:512], ps[:, :]),
                                        waits=[("sPE", vM),
                                               ("sSt", STB + 16 * 4 * max(0, g - 3))],
                                        inc=("sDV", 1), tag="v.ev")
                                    tg = 8 * half + tt
                                    for hh in range(4):
                                        hd0 = 4 * jc + hh
                                        dma("sync",
                                            dscr_t[:, (hd0 * NT + tg) * 128:
                                                   (hd0 * NT + tg) * 128 + 128],
                                            stg[si][:, 128 * hh:128 * hh + 128],
                                            "sSt", waits=[("sDV", vE)], tag="v.st")
                    with contextlib.ExitStack() as s2:
                        lqb = s2.enter_context(
                            nc.sbuf_tensor(nm(), [128, LC * 1024], BF16))
                        emit_proj_fm(wkB, DC, LC, bigA, 1024, 2,
                                     (wb, stg, t1, t2, psP), True, (cosFb, sinFb),
                                     toff, dst_dram=k_scr[:, toff:toff + 1024],
                                     tag="k")
                        emit_proj_fm(wlqB, DC, LC, bigA, 1024, 2,
                                     (wb, stg, t1, t2, psP), False, None, 0,
                                     dst_sbuf=lqb, tag="lq")
                        kb.barrier(f"b.lq{half}")
                        emit_proj_fm(wlkB, LC, DC, lqb, 1024, 2,
                                     (wb, stg, t1, t2, psP), True, (cosFb, sinFb),
                                     toff, dst_dram=lk_scr[:, toff:toff + 1024],
                                     tag="lk")
                        kb.barrier(f"b.kv{half}")
            if not enabled("attn"):
                kb.check(); kb.emit(blk); return nc, kb

        with contextlib.ExitStack() as s34:
            maskS = s34.enter_context(nc.sbuf_tensor("maskS", [128, 16 * 512], BF16))
            kbuf = [s34.enter_context(nc.sbuf_tensor(nm(), [128, S], BF16))
                    for _ in range(2)]
            vbuf = [s34.enter_context(nc.sbuf_tensor(nm(), [128, S], BF16))
                    for _ in range(2)]
            qbuf = [s34.enter_context(nc.sbuf_tensor(nm(), [128, NO], BF16))
                    for _ in range(2)]
            Pb = [s34.enter_context(nc.sbuf_tensor(nm(), [128, 512], BF16))
                  for _ in range(6)]
            PMb = [s34.enter_context(nc.sbuf_tensor(nm(), [128, 512], BF16))
                   for _ in range(6)]
            recS = [s34.enter_context(nc.sbuf_tensor(nm(), [128, 512], F32))
                    for _ in range(2)]
            psS3 = [s34.enter_context(nc.psum_tensor(nm(), [128, 512], F32))
                    for _ in range(4)]
            psPV = [s34.enter_context(nc.psum_tensor(nm(), [128, 512], F32))
                    for _ in range(2)]
            psDen = [s34.enter_context(nc.psum_tensor(nm(), [128, 512], F32))
                     for _ in range(2)]
            dma("sync", maskS[:, :], maskI, "cst", tag="at.m")
            kb.barrier("b.at0")

            att = {"gs": 0, "hc": 0}

            def attention_phase(phase):
                k_src = k_scr if phase == 0 else lk_scr
                v_src = v_scr if phase == 0 else lv_scr
                for h in range(32):
                    hb = h % 2
                    kv = (h // 4) if phase == 0 else h
                    hw = [("aC", max(0, att["gs"] - 24))]
                    vX = None
                    for dst, src in (
                            (kbuf[hb][:, :], k_src[128 * kv:128 * kv + 128, :]),
                            (vbuf[hb][:, :],
                             v_src[:, kv * NT * 128:(kv + 1) * NT * 128]),
                            (qbuf[hb][:, :], q_scr[128 * h:128 * h + 128, :])):
                        vX = dma("sync", dst, src, "sX", waits=hw, tag="at.ld")
                    for cc in range(2):
                        K = SLOT_KT[cc]
                        att["hc"] += 1
                        hc = att["hc"]
                        dp = (hc - 1) % 2
                        plist = []

                        def emit_pvden(kt, K=K, hb=hb, dp=dp, hc=hc):
                            src_b, vsrc = plist[kt]
                            st, sp = (kt == 0), (kt == K - 1)
                            w = [vsrc]
                            if st:
                                w.append(("aE", max(0, hc - 2)))
                            kb.op("tensor",
                                  lambda e, s=src_b,
                                  v=vbuf[hb][:, 128 * kt:128 * kt + 128],
                                  s1=st, s2=sp, d=dp:
                                  e.matmul(psPV[d][:, :], v, s, start=s1, stop=s2),
                                  waits=w, tag="at.pv")
                            kb.op("tensor",
                                  lambda e, s=src_b, s1=st, s2=sp, d=dp:
                                  e.matmul(psDen[d][:, :], onesb[:, :], s,
                                           start=s1, stop=s2),
                                  inc=("aC", 1), tag="at.den")

                        for kt in range(K):
                            att["gs"] += 1
                            gs = att["gs"]
                            bank = (gs - 1) % 4
                            pbank = (gs - 1) % 6
                            vS = kb.op("tensor",
                                       lambda e, ps=psS3[bank][:, :],
                                       l=kbuf[hb][:, 128 * kt:128 * kt + 128],
                                       r=qbuf[hb][:, 512 * cc:512 * cc + 512]:
                                       e.matmul(ps, l, r, start=True, stop=True),
                                       waits=[("sX", vX), ("aP", max(0, gs - 4))],
                                       inc=("aS", 1), tag="at.s")
                            bcol = 8 * cc + kt
                            vP = kb.op("scalar",
                                       lambda e, ps=psS3[bank][:, :],
                                       o=Pb[pbank][:, :], bc=bcol:
                                       e.activation(out=o, in_=ps, func=AF.Exp,
                                                    scale=SCALE,
                                                    bias=biasT[:, bc:bc + 1]),
                                       waits=[("aS", vS), ("aC", max(0, gs - 6))],
                                       inc=("aP", 1), tag="at.e")
                            masked = (cc == 0) or (kt >= 8)
                            if masked:
                                vMk = kb.op("vector",
                                            lambda e, i=Pb[pbank][:, :],
                                            o=PMb[pbank][:, :],
                                            m=maskS[:, 512 * kt:512 * kt + 512]:
                                            e.tensor_mul(o, i, m),
                                            waits=[("aP", vP),
                                                   ("aC", max(0, gs - 6))],
                                            inc=("aM", 1), tag="at.mk")
                                plist.append((PMb[pbank][:, :], ("aM", vMk)))
                            else:
                                plist.append((Pb[pbank][:, :], ("aP", vP)))
                            if kt >= 2:
                                emit_pvden(kt - 2)
                        emit_pvden(K - 2)
                        emit_pvden(K - 1)
                        vCend = kb.val("aC")

                        def epi(e, d=dp, hh=h, ccc=cc, rb=recS[hc % 2]):
                            e.reciprocal(rb[:, :], psDen[d][:, :])
                            e.drain()
                            return e.tensor_mul(
                                bigA[:, 1024 * hh + 512 * ccc:
                                     1024 * hh + 512 * ccc + 512],
                                psPV[d][:, :], rb[:, :])
                        kb.op("vector", epi, waits=[("aC", vCend)],
                              inc=("aE", 1), tag="at.ep")

            def oproj_phase(wsrc, tagp):
                with contextlib.ExitStack() as sc:
                    wb2 = [sc.enter_context(
                        nc.sbuf_tensor(nm(), [128, DC * 128], BF16))
                        for _ in range(3)]
                    ost = [sc.enter_context(nc.sbuf_tensor(nm(), [128, 512], F32))
                           for _ in range(4)]
                    psO = [psS3[0], psS3[1]]
                    vWs = {}
                    for j in range(2):
                        vWs[j] = dma("gpsimd", wb2[j % 3][:, :],
                                     wsrc[128 * j:128 * j + 128, :],
                                     "sW", tag=tagp + ".wpre")
                    kb.barrier("b." + tagp + "0")
                    PEB = kb.val("sPE")
                    DVB = kb.val("sDV")
                    ACB = kb.val("sAc")
                    for j in range(DC):
                        if j + 2 < DC:
                            vWs[j + 2] = dma(
                                "gpsimd", wb2[(j + 2) % 3][:, :],
                                wsrc[128 * (j + 2):128 * (j + 2) + 128, :],
                                "sW", waits=[("sPE", PEB + max(0, j * 2))],
                                tag=tagp + ".w")
                        vW = vWs[j]
                        for tc in range(2):
                            g = j * 2 + tc
                            pb = g % 2
                            vM = None
                            for c in range(DC):
                                st, sp = (c == 0), (c == DC - 1)
                                w = ([("sW", vW), ("sDV", DVB + max(0, g - 1))]
                                     if st else [])
                                fn = (lambda e, l=wb2[j % 3][:, 128 * c:128 * c + 128],
                                      r=bigA[:, 1024 * c + 512 * tc:
                                             1024 * c + 512 * tc + 512],
                                      s1=st, s2=sp, ps=psO[pb][:, :]:
                                      e.matmul(ps, l, r, start=s1, stop=s2))
                                if sp:
                                    vM = kb.op("tensor", fn, waits=w,
                                               inc=("sPE", 1), tag=tagp + ".mm")
                                else:
                                    kb.op("tensor", fn, waits=w, tag=tagp + ".mm")
                            si = g % 4
                            vE = kb.op("vector",
                                       lambda e, ps=psO[pb][:, :], s=ost[si][:, :]:
                                       e.tensor_copy(s, ps),
                                       waits=[("sPE", vM),
                                              ("sAc", ACB + 16 * max(0, g - 3))],
                                       inc=("sDV", 1), tag=tagp + ".ev")
                            dma("gpsimd",
                                outT[128 * j:128 * j + 128, 512 * tc:512 * tc + 512],
                                ost[si][:, :], "sAc", waits=[("sDV", vE)],
                                tag=tagp + ".acc", accum=True)

            attention_phase(0)
            if expose:
                kb.barrier("b.a1")
                for i in range(8):
                    dma("sync", at_dbg[:, 4096 * i:4096 * i + 4096],
                        bigA[:, 4096 * i:4096 * i + 4096], "sSt", tag="dbg.at")
                kb.barrier("b.dbga")
            if enabled("wo"):
                oproj_phase(woB, "wo")
                kb.barrier("b.wo")
            if enabled("lat"):
                attention_phase(1)
                if expose:
                    kb.barrier("b.a2")
                    for i in range(8):
                        dma("sync", lat_dbg[:, 4096 * i:4096 * i + 4096],
                            bigA[:, 4096 * i:4096 * i + 4096], "sSt", tag="dbg.lat")
                    kb.barrier("b.dbgl")
            if enabled("wlo"):
                oproj_phase(wloB, "lo")
                kb.barrier("b.end")

        kb.check()
        kb.emit(blk)
    return nc, kb


def kernel(L0_attn_hidden_states, L0_attn_w_q, L0_attn_w_k, L0_attn_w_v, L0_attn_w_o,
           L0_attn_w_lq, L0_attn_w_lk, L0_attn_w_lv, L0_attn_w_lo,
           L0_ffn_base_W, L0_ffn_base_b, L0_ffn_router_W, L0_ffn_router_b,
           L0_ffn_A, L0_ffn_B, L0_ffn_expert_map):
    global LAST_RES
    x = np.asarray(L0_attn_hidden_states, dtype=np.float32)

    shared = {
        "wqB": _wbands(np.asarray(L0_attn_w_q, np.float32), DC, 32, 128),
        "wkB": _wbands(np.asarray(L0_attn_w_k, np.float32), DC, 8, 128),
        "wvB": _wbands(np.asarray(L0_attn_w_v, np.float32), DC, 2, 512),
        "wlqB": _wbands(np.asarray(L0_attn_w_lq, np.float32), DC, 8, 128),
        "wlkB": _wbands(np.asarray(L0_attn_w_lk, np.float32), LC, 32, 128),
        "wlvB": _wbands(np.asarray(L0_attn_w_lv, np.float32), DC, 8, 512),
        "woB": _wbands(np.asarray(L0_attn_w_o, np.float32), DC, 32, 128),
        "wloB": _wbands(np.asarray(L0_attn_w_lo, np.float32), DC, 32, 128),
        "bWB": _wbands(np.ascontiguousarray(np.asarray(L0_ffn_base_W, np.float32).T),
                       DC, 32, 128),
        "rWB": _wbands(np.ascontiguousarray(np.asarray(L0_ffn_router_W, np.float32).T),
                       DC, 1, NR),
        "ATB": _wbands(np.ascontiguousarray(
            np.asarray(L0_ffn_A, np.float32).reshape(128, D).T), DC, 1, 128),
        "BTB": np.ascontiguousarray(
            np.asarray(L0_ffn_B, np.float32).transpose(0, 2, 1).reshape(128, D)),
        "bbI": np.ascontiguousarray(
            np.asarray(L0_ffn_base_b, np.float32).reshape(32, 128).T),
        "rbI": np.asarray(L0_ffn_router_b, np.float32).reshape(NR, 1),
        "emapI": np.asarray(L0_ffn_expert_map, np.int32).reshape(NR, 1),
        "iotaI": np.tile(np.arange(NE, dtype=np.float32)[None, :], (NR, 1)),
        "p8I": _p8pad(),
        "onesI": np.ones((128, 128), dtype=BF),
        "eyeI": np.eye(128, dtype=np.float32),
    }
    percore = []
    for c in range(8):
        b, p = c // 2, c % 2
        own, cosO, sinO, cosF, sinF, biasA, masks = _host_consts(p)
        xt = np.ascontiguousarray(x[b].T)
        xo = np.ascontiguousarray(xt[:, own])
        m = dict(shared)
        m.update({
            "xoH": _xmajor(xo, NO),
            "xbH": _xmajor(xt, S),
            "cosO": cosO.astype(BF), "sinO": sinO.astype(BF),
            "cosF": np.ascontiguousarray(cosF).astype(BF),
            "sinF": np.ascontiguousarray(sinF).astype(BF),
            "biasI": biasA, "maskI": masks,
        })
        percore.append(m)

    nc, kb = build_program(UPTO, DEV_EXPOSE, SUB2A)
    res = run_bass_kernel_spmd(nc, percore, core_ids=list(range(8)))
    LAST_RES = res

    out = np.empty((B, S, D), dtype=np.float32)
    for c in range(8):
        b, p = c // 2, c % 2
        own = np.concatenate([np.arange(128 * t, 128 * t + 128)
                              for t in OWN_TILES[p]])
        out[b, own, :] = res.results[c]["outT"].T
    return out


# version 08
# speedup vs baseline: 1.1695x; 1.1695x over previous
import contextlib
from collections import defaultdict

import numpy as np
import ml_dtypes
import concourse.bass as bass
import concourse.mybir as mybir
from concourse.bass_utils import run_bass_kernel_spmd

F32 = mybir.dt.float32
BF16 = mybir.dt.bfloat16
I32 = mybir.dt.int32
AF = mybir.ActivationFunctionType
ALU = mybir.AluOpType
AXX = mybir.AxisListType.X
BF = ml_dtypes.bfloat16

B, S, D, HD = 4, 2048, 4096, 128
NH, NKV, LAT = 32, 8, 1024
DC = D // 128
LC = LAT // 128
NT = S // 128
NO = 1024
NR, NE = 15, 8
SCALE = 1.0 / float(np.sqrt(HD))
SHIFT = -12.0
NEG = -3.0e38
EPS = 1e-5
THETA = 10000.0
OWN_TILES = [[0, 1, 2, 3, 12, 13, 14, 15], [4, 5, 6, 7, 8, 9, 10, 11]]
OWN_BASE = [[0, 12], [4, 8]]
SLOT_KT = [8, 16]

ENGS = ("sync", "gpsimd", "tensor", "vector", "scalar")

UPTO = "full"
SUB2A = "all"
DEV_EXPOSE = False
LAST_RES = None


class KB:
    def __init__(self):
        self.ops = {e: [] for e in ENGS}
        self.cnt = defaultdict(int)
        self.sem = {}
        self.last_wait = {}

    def val(self, name):
        return self.cnt[name]

    def op(self, eng, fn, waits=(), inc=None, tag=""):
        w = tuple((s, v) for (s, v) in waits if v > 0)
        v = None
        if inc is not None:
            self.cnt[inc[0]] += inc[1]
            v = self.cnt[inc[0]]
        self.ops[eng].append({"fn": fn, "waits": w, "inc": inc, "tag": tag})
        return v

    def barrier(self, tag="bar"):
        snap = [(s, v) for s, v in self.cnt.items() if v > 0]
        for e in ENGS:
            self.op(e, None, waits=snap, tag=tag)

    def check(self):
        ptr = {e: 0 for e in ENGS}
        semv = defaultdict(int)
        total = sum(len(v) for v in self.ops.values())
        done = 0
        progress = True
        while progress:
            progress = False
            for e in ENGS:
                while ptr[e] < len(self.ops[e]):
                    o = self.ops[e][ptr[e]]
                    if all(semv[s] >= v for s, v in o["waits"]):
                        if o["inc"] is not None:
                            semv[o["inc"][0]] += o["inc"][1]
                        ptr[e] += 1
                        done += 1
                        progress = True
                    else:
                        break
        if done < total:
            stuck = {}
            for e in ENGS:
                if ptr[e] < len(self.ops[e]):
                    o = self.ops[e][ptr[e]]
                    blocked = [(s, v, semv[s]) for s, v in o["waits"] if semv[s] < v]
                    stuck[e] = (ptr[e], o["tag"], blocked)
            raise RuntimeError(f"DEADLOCK: {stuck}")
        for e in ENGS:
            for o in self.ops[e]:
                for s, v in o["waits"]:
                    assert v <= self.cnt[s], \
                        f"wait {s}>={v} but total {self.cnt[s]} ({o['tag']})"

    def emit(self, block):
        def mk(ename):
            def body(eng):
                for o in self.ops[ename]:
                    for s, v in o["waits"]:
                        k = (ename, s)
                        if self.last_wait.get(k, 0) >= v:
                            continue
                        self.last_wait[k] = v
                        eng.wait_ge(self.sem[s], v)
                    if o["fn"] is not None:
                        r = o["fn"](eng)
                        if o["inc"] is not None:
                            s, by = o["inc"]
                            r.then_inc(self.sem[s], by)
            return body
        for ename, meth in (("sync", block.sync), ("gpsimd", block.gpsimd),
                            ("tensor", block.tensor), ("vector", block.vector),
                            ("scalar", block.scalar)):
            if self.ops[ename]:
                meth(mk(ename))


def _rope_tables():
    inv = THETA ** (-np.arange(0, HD, 2, dtype=np.float64) / HD)
    t = np.arange(S, dtype=np.float64)
    fr = np.outer(t, inv)
    emb = np.concatenate([fr, fr], axis=1)
    cos = np.cos(emb).T.astype(np.float32)
    sin = np.sin(emb).T.astype(np.float32)
    ssgn = sin.copy()
    ssgn[0:64, :] = -ssgn[0:64, :]
    return cos, ssgn


def _host_consts(parity):
    cos, ssgn = _rope_tables()
    own = np.concatenate([np.arange(128 * t, 128 * t + 128) for t in OWN_TILES[parity]])
    cosO = np.ascontiguousarray(cos[:, own])
    sinO = np.ascontiguousarray(ssgn[:, own])
    bias = np.full((128, 24), NEG, dtype=np.float32)
    masks = np.ones((128, 16, 512), dtype=np.float32)
    for cc in range(2):
        base = OWN_BASE[parity][cc]
        for kt in range(SLOT_KT[cc]):
            if kt <= base + 3:
                bias[:, 8 * cc + kt] = SHIFT
        for kt in (range(8) if cc == 0 else range(8, 16)):
            if base <= kt <= base + 3:
                j = kt - base
                p = np.arange(128)[:, None]
                c = np.arange(512)[None, :]
                masks[:, kt, :] = (c >= 128 * j + p).astype(np.float32)
    masks = np.ascontiguousarray(masks.reshape(128, 16 * 512)).astype(BF)
    return own, cosO, sinO, cos, ssgn, bias, masks


def _p8pad():
    p = np.zeros((128, 128), dtype=np.float32)
    for e in range(NE):
        p[e, e * 16:(e + 1) * 16] = 2.0
    return p


def _wbands(w, nch, nj, ncol):
    return np.ascontiguousarray(
        w.reshape(nch, 128, nj, ncol).transpose(2, 1, 0, 3).reshape(nj * 128, nch * ncol))


def _xmajor(xt, ncols):
    return np.ascontiguousarray(
        xt.reshape(32, 128, ncols).transpose(1, 0, 2).reshape(128, 32 * ncols))


def build_program(upto=UPTO, expose=DEV_EXPOSE, sub2a=None):
    if sub2a is None:
        sub2a = SUB2A
    nc = bass.Bass("TRN2", target_bir_lowering=False, debug=False)
    kb = KB()
    SEM = kb.sem
    phases = ["ln", "ffn", "q", "kv", "attn", "wo", "lat", "wlo", "full"]
    plim = phases.index(upto)

    def enabled(ph):
        return phases.index(ph) <= plim

    def din(name, shape, dt=F32):
        return nc.dram_tensor(name, shape, dt, kind="ExternalInput").ap()

    def dscr(name, shape, dt=BF16):
        kind = "ExternalOutput" if expose else "Internal"
        return nc.dram_tensor(name, shape, dt, kind=kind).ap()

    xoH = din("xoH", [128, DC * NO])
    xbH = din("xbH", [128, DC * S])
    wqB = din("wqB", [32 * 128, DC * 128])
    wkB = din("wkB", [8 * 128, DC * 128])
    wvB = din("wvB", [2 * 128, DC * 512])
    wlqB = din("wlqB", [8 * 128, DC * 128])
    wlkB = din("wlkB", [32 * 128, LC * 128])
    wlvB = din("wlvB", [8 * 128, DC * 512])
    woB = din("woB", [32 * 128, DC * 128])
    wloB = din("wloB", [32 * 128, DC * 128])
    bWB = din("bWB", [32 * 128, DC * 128])
    rWB = din("rWB", [128, DC * NR])
    ATB = din("ATB", [128, DC * 128])
    BTB = din("BTB", [128, D])
    bbI = din("bbI", [128, 32])
    rbI = din("rbI", [NR, 1])
    emapI = din("emapI", [NR, 1], I32)
    iotaI = din("iotaI", [NR, NE])
    p8I = din("p8I", [128, 128])
    onesI = din("onesI", [128, 128], BF16)
    eyeI = din("eyeI", [128, 128])
    cosO_I = din("cosO", [128, NO], BF16)
    sinO_I = din("sinO", [128, NO], BF16)
    cosF_I = din("cosF", [128, S], BF16)
    sinF_I = din("sinF", [128, S], BF16)
    biasI = din("biasI", [128, 24])
    maskI = din("maskI", [128, 16 * 512], BF16)

    outT = nc.dram_tensor("outT", [D, NO], F32, kind="ExternalOutput").ap()
    if expose:
        xn_dbg = dscr("xn_dbg", [128, DC * 1024])
        hw_dbg = dscr("hw_dbg", [128, NO])
        lg_dbg = dscr("lg_dbg", [NR, NO], F32)
        mp_dbg = dscr("mp_dbg", [NR, NO], F32)
        at_dbg = dscr("at_dbg", [128, DC * 1024])
        lat_dbg = dscr("lat_dbg", [128, DC * 1024])
    q_scr = dscr("q_scr", [D, NO])
    k_scr = dscr("k_scr", [LAT, S])
    lk_scr = dscr("lk_scr", [D, S])
    v_scr = dscr("v_scr", [128, NKV * NT * 128])
    lv_scr = dscr("lv_scr", [128, NH * NT * 128])

    big = contextlib.ExitStack()
    with big:
        for n in ["cst", "sW0", "sW1", "sX0", "sX1",
                  "sSt0", "sSt1", "sSt2", "sSt3",
                  "sAc0", "sAc1", "sAc2", "sAc3",
                  "oW0", "oW1", "oW2", "sPE", "sDV", "sAC",
                  "aS", "aP", "aM", "aC", "aE"]:
            SEM[n] = big.enter_context(nc.semaphore(n))

        onesb = big.enter_context(nc.sbuf_tensor("onesb", [128, 128], BF16))
        eyeS = big.enter_context(nc.sbuf_tensor("eyeS", [128, 128], F32))
        biasT = big.enter_context(nc.sbuf_tensor("biasT", [128, 24], F32))
        bbS = big.enter_context(nc.sbuf_tensor("bbS", [128, 32], F32))
        epsT = big.enter_context(nc.sbuf_tensor("epsT", [128, 1], F32))
        hwT = big.enter_context(nc.sbuf_tensor("hwT", [128, NO], BF16))
        bigA = big.enter_context(nc.sbuf_tensor("bigA", [128, DC * 1024], BF16))

        blk = big.enter_context(nc.Block())

        _ctr = [0]

        def nm():
            _ctr[0] += 1
            return f"t{_ctr[0]}"

        def dma(eng, dst, src, sem, waits=(), tag="", accum=False):
            if accum:
                fn = lambda e, d=dst, s=src: e.dma_start(out=d, in_=s, accum_op=ALU.add)
            else:
                fn = lambda e, d=dst, s=src: e.dma_start(out=d, in_=s)
            v = kb.op(eng, fn, waits=waits, inc=(sem, 16), tag=tag)
            return (sem, v)

        for dst, src in ((onesb[:, :], onesI), (eyeS[:, :], eyeI), (biasT[:, :], biasI),
                         (bbS[:, :], bbI)):
            dma("sync", dst, src, "cst", tag="cst")
        kb.op("vector", lambda e: e.memset(epsT[:, :], EPS), tag="eps")
        kb.op("vector", lambda e: e.drain(), tag="eps.dr")
        kb.op("vector", lambda e: e.tensor_copy(epsT[0:1, 0:1], epsT[0:1, 0:1]),
              inc=("sDV", 1), tag="eps.ms")
        kb.barrier("b.cst")

        def alloc_ln(sc):
            CW = 256
            b = {"CW": CW}
            b["xbf"] = [sc.enter_context(nc.sbuf_tensor(nm(), [128, DC * CW], BF16))
                        for _ in range(2)]
            b["sqr"] = [sc.enter_context(nc.sbuf_tensor(nm(), [128, CW], BF16))
                        for _ in range(4)]
            for k in ["negm", "qm", "mm2", "varb", "sdb"]:
                b[k] = sc.enter_context(nc.sbuf_tensor(nm(), [128, CW], F32))
            b["rstd"] = [sc.enter_context(nc.sbuf_tensor(nm(), [128, CW], F32))
                         for _ in range(2)]
            b["bb"] = [sc.enter_context(nc.sbuf_tensor(nm(), [128, CW], F32))
                       for _ in range(2)]
            b["tmp"] = sc.enter_context(nc.sbuf_tensor(nm(), [128, CW], F32))
            b["psS"] = [sc.enter_context(nc.psum_tensor(nm(), [128, CW], F32))]
            b["psQ"] = [sc.enter_context(nc.psum_tensor(nm(), [128, CW], F32))]
            return b

        def emit_ln(b, src_cols, ntc, dst, dst_off):
            CW = b["CW"]
            applies = {}
            stats_done = {}
            sqrts = {}
            for i in range(ntc):
                xb2 = b["xbf"][i % 2]
                psS, psQ = b["psS"][0], b["psQ"][0]
                rstd, bbb = b["rstd"][i % 2], b["bb"][i % 2]
                wv = dma("gpsimd", xb2[:, :].rearrange("p (c n) -> p c n", n=CW),
                         src_cols(i), f"sW{i % 2}",
                         waits=[("sDV", applies.get(i - 2, 0))], tag="ln.x")
                pe0 = kb.val("sPE")
                for c in range(DC):
                    vA = kb.op("scalar",
                               lambda e, o=b["sqr"][c % 4][:, :],
                               i_=xb2[:, CW * c:CW * c + CW]:
                               e.activation(out=o, in_=i_, func=AF.Square),
                               waits=[wv, ("sPE", pe0 + (c - 3))],
                               inc=("sAC", 1), tag="ln.sq")
                    kb.op("tensor",
                          lambda e, i_=xb2[:, CW * c:CW * c + CW],
                          st=(c == 0), sp=(c == DC - 1), ps=psS:
                          e.matmul(ps[:, :], onesb[:, :], i_, start=st, stop=sp),
                          waits=([wv, ("sDV", stats_done.get(i - 1, 0))]
                                 if c == 0 else []),
                          tag="ln.s")
                    kb.op("tensor",
                          lambda e, i_=b["sqr"][c % 4][:, :],
                          st=(c == 0), sp=(c == DC - 1), ps=psQ:
                          e.matmul(ps[:, :], onesb[:, :], i_, start=st, stop=sp),
                          waits=[("sAC", vA)], inc=("sPE", 1), tag="ln.q")
                pe_done = kb.val("sPE")

                def stats(e, b=b, psS=psS, psQ=psQ):
                    e.tensor_scalar_mul(b["negm"][:, :], psS[:, :], -1.0 / D)
                    e.tensor_scalar_mul(b["qm"][:, :], psQ[:, :], 1.0 / D)
                    e.drain()
                    e.tensor_mul(b["mm2"][:, :], b["negm"][:, :], b["negm"][:, :])
                    e.drain()
                    return e.tensor_sub(b["varb"][:, :], b["qm"][:, :], b["mm2"][:, :])
                vD = kb.op("vector", stats,
                           waits=[("sPE", pe_done), ("sAC", sqrts.get(i - 1, 0))],
                           inc=("sDV", 1), tag="ln.st")
                stats_done[i] = vD
                vA2 = kb.op("scalar",
                            lambda e, b=b: e.activation(
                                out=b["sdb"][:, :], in_=b["varb"][:, :],
                                func=AF.Sqrt, bias=epsT[:, 0:1]),
                            waits=[("sDV", vD)], inc=("sAC", 1), tag="ln.sqrt")
                sqrts[i] = vA2

                def fin(e, b=b, rstd=rstd, bbb=bbb):
                    e.reciprocal(rstd[:, :], b["sdb"][:, :])
                    e.drain()
                    return e.tensor_mul(bbb[:, :], b["negm"][:, :], rstd[:, :])
                vD2 = kb.op("vector", fin, waits=[("sAC", vA2)],
                            inc=("sDV", 1), tag="ln.fin")

                def apply(e, b=b, ii=i, xb2=xb2, rstd=rstd, bbb=bbb):
                    for c in range(DC):
                        e.tensor_mul(b["tmp"][:, :], xb2[:, CW * c:CW * c + CW],
                                     rstd[:, :])
                        e.drain()
                        e.tensor_add(dst[:, 1024 * c + dst_off + CW * ii:
                                         1024 * c + dst_off + CW * ii + CW],
                                     b["tmp"][:, :], bbb[:, :])
                        e.drain()
                    return e.tensor_copy(b["tmp"][0:1, 0:1], b["tmp"][0:1, 0:1])
                applies[i] = kb.op("vector", apply, waits=[("sDV", vD2)],
                                   inc=("sDV", 1), tag="ln.ap")

        def emit_proj_fm(wsrc, nch, njs, rhs, rhs_stride, ntc, bufs, rope, cos_sin,
                         cos_off, dst_dram=None, dst_sbuf=None, tag="pj"):
            wb, stg, t1, t2, psP = bufs
            PEB = kb.val("sPE")
            DVB = kb.val("sDV")
            prev_store = {}
            for ji in range(njs):
                wv = dma("gpsimd", wb[ji % 2][:, 0:nch * 128],
                         wsrc[128 * ji:128 * ji + 128, :],
                         f"sW{ji % 2}", waits=[("sPE", PEB + max(0, (ji - 1) * ntc))],
                         tag=tag + ".w")
                for tc in range(ntc):
                    g = ji * ntc + tc
                    pb = g % 3
                    vM = None
                    for c in range(nch):
                        st, sp = (c == 0), (c == nch - 1)
                        w = [wv, ("sDV", DVB + max(0, g - 2))] if st else []
                        fn = (lambda e, ps=psP[pb][:, :],
                              l=wb[ji % 2][:, 128 * c:128 * c + 128],
                              r=rhs[:, rhs_stride * c + 512 * tc:
                                    rhs_stride * c + 512 * tc + 512],
                              s1=st, s2=sp: e.matmul(ps, l, r, start=s1, stop=s2))
                        if sp:
                            vM = kb.op("tensor", fn, waits=w, inc=("sPE", 1),
                                       tag=f"{tag}.mm")
                        else:
                            kb.op("tensor", fn, waits=w, tag=f"{tag}.mm")
                    si = g % 4
                    stw = [("sPE", vM)]
                    if dst_dram is not None and si in prev_store:
                        stw.append(prev_store[si])
                    if rope:
                        cosb, sinb = cos_sin
                        co = cos_off + 512 * tc

                        def evac(e, ps=psP[pb], s=stg[si], co=co):
                            e.tensor_mul(t1[:, :], ps[:, :], cosb[:, co:co + 512])
                            e.tensor_mul(t2[0:64, :], ps[64:128, :],
                                         sinb[0:64, co:co + 512])
                            e.tensor_mul(t2[64:128, :], ps[0:64, :],
                                         sinb[64:128, co:co + 512])
                            e.drain()
                            return e.tensor_add(s[:, :], t1[:, :], t2[:, :])
                    elif dst_dram is not None:
                        def evac(e, ps=psP[pb], s=stg[si]):
                            return e.tensor_copy(s[:, :], ps[:, :])
                    else:
                        def evac(e, ps=psP[pb], jj=ji, tcc=tc):
                            return e.tensor_copy(
                                dst_sbuf[:, 1024 * jj + 512 * tcc:
                                         1024 * jj + 512 * tcc + 512], ps[:, :])
                    vE = kb.op("vector", evac, waits=stw, inc=("sDV", 1),
                               tag=f"{tag}.ev")
                    if dst_dram is not None:
                        prev_store[si] = dma(
                            "sync",
                            dst_dram[128 * ji:128 * ji + 128, 512 * tc:512 * tc + 512],
                            stg[si][:, :], f"sSt{si}", waits=[("sDV", vE)],
                            tag=tag + ".st")

        with contextlib.ExitStack() as sc:
            lb = alloc_ln(sc)
            xo3 = xoH.rearrange("p (c n) -> p c n", n=NO)
            emit_ln(lb, lambda tc: xo3[:, :, 256 * tc:256 * tc + 256], NO // 256,
                    bigA, 0)
            kb.barrier("b.ln")
            if expose:
                for i in range(8):
                    dma("sync", xn_dbg[:, 4096 * i:4096 * i + 4096],
                        bigA[:, 4096 * i:4096 * i + 4096], "sSt0", tag="dbg.xn")
                kb.barrier("b.dbg1")
        if not enabled("ffn"):
            kb.check(); kb.emit(blk); return nc, kb

        with contextlib.ExitStack() as sc:
            E = lambda n, sh, dt=F32: sc.enter_context(nc.sbuf_tensor(n, sh, dt))
            rWTb = E("rWTb", [128, DC * NR], BF16)
            ATb = E("ATb", [128, DC * 128], BF16)
            mpS = E("mpS", [128, NO])
            wsb = E("wsb", [128, 512])
            wexpS = E("wexpS", [128, 512])
            EmapP = E("EmapP", [128, NE])
            p8S = E("p8S", [128, 128])
            iotaS = E("iotaS", [NR, NE])
            emapS = E("emapS", [NR, 1], I32)
            emapF = E("emapF", [NR, 1])
            rbS = E("rbS", [NR, 1])
            lgf = E("lgf", [NR, NO])
            ltok = E("ltok", [128, NR])
            eo = E("eo", [128, NR])
            nmax = E("nmax", [128, 1])
            sume = E("sume", [128, 1])
            recp = E("recp", [128, 1])
            beat = E("beat", [128, NR])
            eqc = E("eqc", [128, NR])
            tot = E("tot", [128, NR])
            mps = E("mps", [128, NR])
            junk = E("junk", [128, NR])
            psR = sc.enter_context(nc.psum_tensor("psR", [128, 512], F32))
            psT = sc.enter_context(nc.psum_tensor("psT", [128, 128], F32))
            psW = sc.enter_context(nc.psum_tensor("psW", [128, 512], F32))
            psH = sc.enter_context(nc.psum_tensor("psH", [128, 512], F32))

            vW1 = dma("gpsimd", rWTb[:, :], rWB, "sW0", tag="rt.w")
            vW2 = dma("gpsimd", ATb[:, :], ATB, "sW1", tag="rt.at")
            for dst, src in ((p8S[:, :], p8I), (iotaS[:, :], iotaI),
                             (emapS[:, :], emapI), (rbS[:, :], rbI)):
                dma("sync", dst, src, "cst", tag="rt.c")
            CC = kb.val("cst")

            def prep(e):
                e.tensor_copy(emapF[:, :], emapS[:, :])
                e.memset(EmapP[:, :], 0.0)
                e.drain()
                e.tensor_single_scalar(EmapP[0:NR, :], iotaS[:, :], emapF[:, 0:1],
                                       ALU.is_equal)
                e.drain()
                return e.tensor_copy(EmapP[0:1, 0:1], EmapP[0:1, 0:1])
            vPrep = kb.op("vector", prep, waits=[("cst", CC)], inc=("sDV", 1),
                          tag="rt.prep")

            for tc in range(2):
                vP = None
                for c in range(DC):
                    st, sp = (c == 0), (c == DC - 1)
                    w = [vW1, ("sDV", kb.val("sDV"))] if st else []
                    fn = (lambda e, l=rWTb[:, NR * c:NR * c + NR],
                          r=bigA[:, 1024 * c + 512 * tc:1024 * c + 512 * tc + 512],
                          s1=st, s2=sp:
                          e.matmul(psR[0:NR, :], l, r, start=s1, stop=s2))
                    if sp:
                        vP = kb.op("tensor", fn, waits=w, inc=("sPE", 1), tag="rt.mm")
                    else:
                        kb.op("tensor", fn, waits=w, tag="rt.mm")
                kb.op("vector",
                      lambda e, d=lgf[:, 512 * tc:512 * tc + 512]:
                      e.tensor_scalar_add(d, psR[0:NR, 0:512], rbS[:, 0:1]),
                      waits=[("sPE", vP), ("cst", CC)], inc=("sDV", 1), tag="rt.lg")
            LGD = kb.val("sDV")
            if expose and sub2a in ("lg", "tk"):
                dma("sync", lg_dbg, lgf[0:NR, :], "sSt0",
                    waits=[("sDV", LGD)], tag="dbg.lg")
            if sub2a == "lg":
                kb.barrier("b.rt")
                kb.check(); kb.emit(blk); return nc, kb

            for tt in range(NO // 128):
                vT = kb.op("tensor",
                           lambda e, i=lgf[0:NR, 128 * tt:128 * tt + 128]:
                           e.transpose(psT[:, 0:NR], i, eyeS[0:NR, 0:NR]),
                           waits=[("sDV", LGD), ("sDV", kb.val("sDV"))],
                           inc=("sPE", 1), tag="tk.tr")

                def tokstat(e):
                    e.tensor_copy(ltok[:, :], psT[:, 0:NR])
                    e.drain()
                    e.reduce_max(nmax[:, :], ltok[:, :], axis=AXX, negate=True)
                    e.drain()
                    return e.tensor_copy(nmax[0:1, 0:1], nmax[0:1, 0:1])
                vDm = kb.op("vector", tokstat, waits=[("sPE", vT)],
                            inc=("sDV", 1), tag="tk.st")
                vAe = kb.op("scalar",
                            lambda e: e.activation(out=eo[:, :], in_=ltok[:, :],
                                                   func=AF.Exp, bias=nmax[:, 0:1],
                                                   accum_out=sume[:, :]),
                            waits=[("sDV", vDm)], inc=("sAC", 1), tag="tk.exp")

                def select(e):
                    e.reciprocal(recp[:, :], sume[:, :])
                    e.memset(eqc[:, :], 0.0)
                    for i in range(NR):
                        e.tensor_scalar(junk[:, :], eo[:, :], eo[:, i:i + 1], None,
                                        ALU.is_gt, op1=ALU.add,
                                        accum_out=beat[:, i:i + 1])
                    for i in range(1, NR):
                        e.tensor_scalar(junk[:, 0:i], eo[:, 0:i], eo[:, i:i + 1], None,
                                        ALU.is_equal, op1=ALU.add,
                                        accum_out=eqc[:, i:i + 1])
                    e.drain()
                    e.tensor_add(tot[:, :], beat[:, :], eqc[:, :])
                    e.drain()
                    e.tensor_single_scalar(tot[:, :], tot[:, :], 7.5, ALU.is_lt)
                    e.drain()
                    e.tensor_mul(mps[:, :], tot[:, :], eo[:, :])
                    e.drain()
                    e.tensor_scalar_mul(mps[:, :], mps[:, :], recp[:, 0:1])
                    e.drain()
                    return e.tensor_copy(mps[0:1, 0:1], mps[0:1, 0:1])
                vDs = kb.op("vector", select, waits=[("sAC", vAe)],
                            inc=("sDV", 1), tag="tk.sel")
                vT2 = kb.op("tensor",
                            lambda e: e.transpose(psT[0:NR, 0:128], mps[:, :],
                                                  eyeS[:, :]),
                            waits=[("sDV", vDs)], inc=("sPE", 1), tag="tk.tr2")
                kb.op("vector",
                      lambda e, d=mpS[0:NR, 128 * tt:128 * tt + 128]:
                      e.tensor_copy(d, psT[0:NR, 0:128]),
                      waits=[("sPE", vT2)], inc=("sDV", 1), tag="tk.cp")

            if expose and sub2a == "tk":
                dma("sync", mp_dbg, mpS[0:NR, :], "sSt0",
                    waits=[("sDV", kb.val("sDV"))], tag="dbg.mp")
            if sub2a == "tk":
                kb.barrier("b.rt")
                kb.check(); kb.emit(blk); return nc, kb

            for tc in range(2):
                vP1 = kb.op("tensor",
                            lambda e, r=mpS[0:NR, 512 * tc:512 * tc + 512]:
                            e.matmul(psW[0:NE, 0:512], EmapP[0:NR, 0:NE], r,
                                     start=True, stop=True),
                            waits=[("sDV", kb.val("sDV")), ("sDV", vPrep)],
                            inc=("sPE", 1), tag="lw.w")
                vD1 = kb.op("vector",
                            lambda e: e.tensor_copy(wsb[0:NE, :], psW[0:NE, 0:512]),
                            waits=[("sPE", vP1)], inc=("sDV", 1), tag="lw.cp")
                vP2 = kb.op("tensor",
                            lambda e: e.matmul(psW[:, :], p8S[0:NE, :], wsb[0:NE, :],
                                               start=True, stop=True),
                            waits=[("sDV", vD1), ("cst", CC)],
                            inc=("sPE", 1), tag="lw.ex")
                vD2 = kb.op("vector",
                            lambda e: e.tensor_copy(wexpS[:, :], psW[:, :]),
                            waits=[("sPE", vP2)], inc=("sDV", 1), tag="lw.excp")
                vPh = None
                for c in range(DC):
                    st, sp = (c == 0), (c == DC - 1)
                    w = [vW2, ("sDV", vD2)] if st else []
                    fn = (lambda e, l=ATb[:, 128 * c:128 * c + 128],
                          r=bigA[:, 1024 * c + 512 * tc:1024 * c + 512 * tc + 512],
                          s1=st, s2=sp:
                          e.matmul(psH[:, :], l, r, start=s1, stop=s2))
                    if sp:
                        vPh = kb.op("tensor", fn, waits=w, inc=("sPE", 1), tag="lw.h")
                    else:
                        kb.op("tensor", fn, waits=w, tag="lw.h")
                kb.op("vector",
                      lambda e, d=hwT[:, 512 * tc:512 * tc + 512]:
                      e.tensor_mul(d, psH[:, :], wexpS[:, :]),
                      waits=[("sPE", vPh)], inc=("sDV", 1), tag="lw.hw")
            if expose:
                dma("sync", hw_dbg[:, :], hwT[:, :], "sSt0",
                    waits=[("sDV", kb.val("sDV"))], tag="dbg.hw")
            kb.barrier("b.rt")
        if not enabled("q"):
            kb.check(); kb.emit(blk); return nc, kb

        with contextlib.ExitStack() as scp:
            wb = [scp.enter_context(nc.sbuf_tensor(nm(), [128, DC * 128], BF16))
                  for _ in range(2)]
            stg = [scp.enter_context(nc.sbuf_tensor(nm(), [128, 512], BF16))
                   for _ in range(4)]
            t1 = scp.enter_context(nc.sbuf_tensor(nm(), [128, 512], F32))
            t2 = scp.enter_context(nc.sbuf_tensor(nm(), [128, 512], F32))
            psP = [scp.enter_context(nc.psum_tensor(nm(), [128, 512], F32))
                   for _ in range(3)]
            scq = contextlib.ExitStack()
            cosOb = scq.enter_context(nc.sbuf_tensor("cosOb", [128, NO], BF16))
            sinOb = scq.enter_context(nc.sbuf_tensor("sinOb", [128, NO], BF16))
            for dst, src in ((cosOb[:, :], cosO_I), (sinOb[:, :], sinO_I)):
                dma("sync", dst, src, "cst", tag="ro.c")
            kb.barrier("b.q0")

            emit_proj_fm(wqB, DC, 32, bigA, 1024, 2, (wb, stg, t1, t2, psP),
                         True, (cosOb, sinOb), 0, dst_dram=q_scr, tag="q")
            kb.barrier("b.q")
            scq.close()

            with contextlib.ExitStack() as sc:
                BTIb = sc.enter_context(nc.sbuf_tensor("BTIb", [128, D], BF16))
                xot = [sc.enter_context(nc.sbuf_tensor(nm(), [128, 512], F32))
                       for _ in range(4)]
                ost = [sc.enter_context(nc.sbuf_tensor(nm(), [128, 512], F32))
                       for _ in range(4)]
                vWB = dma("gpsimd", BTIb[:, :], BTB, "sW0", tag="fo.bt")
                PEB = kb.val("sPE")
                DVB = kb.val("sDV")
                prev_st = {}
                for j in range(DC):
                    wv = dma("gpsimd", wb[j % 2][:, :],
                             bWB[128 * j:128 * j + 128, :],
                             f"sW{j % 2}", waits=[("sPE", PEB + max(0, (j - 1) * 2))],
                             tag="fo.w")
                    for tc in range(2):
                        g = j * 2 + tc
                        pb = g % 3
                        vX = dma("sync", xot[g % 4][:, :],
                                 xoH[:, j * NO + 512 * tc:j * NO + 512 * tc + 512],
                                 f"sX{g % 2}", waits=[("sDV", DVB + max(0, g - 3))],
                                 tag="fo.x")
                        for c in range(DC):
                            st = (c == 0)
                            w = ([wv, ("sDV", DVB + max(0, g - 2))]
                                 if st else [])
                            fn = (lambda e, l=wb[j % 2][:, 128 * c:128 * c + 128],
                                  r=bigA[:, 1024 * c + 512 * tc:
                                         1024 * c + 512 * tc + 512],
                                  s1=st, ps=psP[pb][:, :]:
                                  e.matmul(ps, l, r, start=s1, stop=False))
                            kb.op("tensor", fn, waits=w, tag="fo.mm")
                        vM = kb.op("tensor",
                                   lambda e, l=BTIb[:, 128 * j:128 * j + 128],
                                   r=hwT[:, 512 * tc:512 * tc + 512],
                                   ps=psP[pb][:, :]:
                                   e.matmul(ps, l, r, start=False, stop=True),
                                   waits=[vWB], inc=("sPE", 1), tag="fo.lb")
                        si = g % 4

                        def evac(e, ps=psP[pb], s=ost[si], jj=j, xo=xot[g % 4]):
                            e.tensor_scalar_add(t1[:, :], ps[:, :], bbS[:, jj:jj + 1])
                            e.drain()
                            return e.tensor_add(s[:, :], t1[:, :], xo[:, :])
                        fw = [("sPE", vM), vX]
                        if si in prev_st:
                            fw.append(prev_st[si])
                        vE = kb.op("vector", evac, waits=fw,
                                   inc=("sDV", 1), tag="fo.ev")
                        prev_st[si] = dma(
                            "sync",
                            outT[128 * j:128 * j + 128, 512 * tc:512 * tc + 512],
                            ost[si][:, :], f"sSt{si}", waits=[("sDV", vE)],
                            tag="fo.st")
                kb.barrier("b.fo")
            if not enabled("kv"):
                kb.check(); kb.emit(blk); return nc, kb

            with contextlib.ExitStack() as sc:
                cosFb = sc.enter_context(nc.sbuf_tensor("cosFb", [128, S], BF16))
                sinFb = sc.enter_context(nc.sbuf_tensor("sinFb", [128, S], BF16))
                for dst, src in ((cosFb[:, :], cosF_I), (sinFb[:, :], sinF_I)):
                    dma("sync", dst, src, "cst", tag="kv.c")
                xb3 = xbH.rearrange("p (c n) -> p c n", n=S)
                wvb = [sc.enter_context(nc.sbuf_tensor(nm(), [128, DC * 512], BF16))
                       for _ in range(2)]
                psV = [sc.enter_context(nc.psum_tensor(nm(), [128, 512], F32))
                       for _ in range(2)]
                vpre = {}
                vlast_pe = [0, 0]
                for half in range(2):
                    toff = 1024 * half
                    for jc in range(2):
                        wsrc0 = wvB
                        vpre[(half, jc)] = dma(
                            "gpsimd", wvb[jc][:, :],
                            wsrc0[128 * jc:128 * jc + 128, :], f"sW{jc}",
                            waits=[("sPE", vlast_pe[jc])], tag="v.wpre")
                    with contextlib.ExitStack() as s2:
                        lb = alloc_ln(s2)
                        emit_ln(lb,
                                lambda tc, toff=toff:
                                xb3[:, :, toff + 256 * tc:toff + 256 * tc + 256],
                                4, bigA, 0)
                        kb.barrier(f"b.kvln{half}")
                    with contextlib.ExitStack() as s2:
                        PEB = kb.val("sPE")
                        DVB = kb.val("sDV")
                        vprev_st = {}
                        gall = [0]
                        for wsrc, njc, dscr_t, nheads in ((wvB, 2, v_scr, NKV),
                                                          (wlvB, 8, lv_scr, NH)):
                            for jc in range(njc):
                                if wsrc is wvB:
                                    wv = vpre[(half, jc)]
                                else:
                                    wv = dma("gpsimd", wvb[jc % 2][:, :],
                                             wsrc[128 * jc:128 * jc + 128, :],
                                             f"sW{jc % 2}",
                                             waits=[("sPE",
                                                     PEB + max(0, gall[0] - 8))],
                                             tag="v.w")
                                if wsrc is wlvB and jc >= 6:
                                    vlast_pe[jc % 2] = PEB + 8 * (jc + 1)
                                for tt in range(8):
                                    g = gall[0]
                                    gall[0] += 1
                                    pb = g % 2
                                    vM = None
                                    for c in range(DC):
                                        st, sp = (c == 0), (c == DC - 1)
                                        w = ([wv,
                                              ("sDV", DVB + max(0, g - 1))]
                                             if st else [])
                                        fn = (lambda e,
                                              l=bigA[:, 1024 * c + 128 * tt:
                                                     1024 * c + 128 * tt + 128],
                                              r=wvb[jc % 2][:, 512 * c:512 * c + 512],
                                              s1=st, s2=sp, ps=psV[pb][:, :]:
                                              e.matmul(ps, l, r, start=s1, stop=s2))
                                        if sp:
                                            vM = kb.op("tensor", fn, waits=w,
                                                       inc=("sPE", 1), tag="v.mm")
                                        else:
                                            kb.op("tensor", fn, waits=w, tag="v.mm")
                                    si = g % 4
                                    vw2 = [("sPE", vM)]
                                    if si in vprev_st:
                                        vw2.append(vprev_st[si])
                                    vE = kb.op(
                                        "vector",
                                        lambda e, ps=psV[pb], s=stg[si]:
                                        e.tensor_copy(s[:, 0:512], ps[:, :]),
                                        waits=vw2, inc=("sDV", 1), tag="v.ev")
                                    tg = 8 * half + tt
                                    for hh in range(4):
                                        hd0 = 4 * jc + hh
                                        vprev_st[si] = dma(
                                            "sync",
                                            dscr_t[:, (hd0 * NT + tg) * 128:
                                                   (hd0 * NT + tg) * 128 + 128],
                                            stg[si][:, 128 * hh:128 * hh + 128],
                                            f"sSt{si}", waits=[("sDV", vE)],
                                            tag="v.st")
                    with contextlib.ExitStack() as s2:
                        lqb = s2.enter_context(
                            nc.sbuf_tensor(nm(), [128, LC * 1024], BF16))
                        emit_proj_fm(wkB, DC, LC, bigA, 1024, 2,
                                     (wb, stg, t1, t2, psP), True, (cosFb, sinFb),
                                     toff, dst_dram=k_scr[:, toff:toff + 1024],
                                     tag="k")
                        emit_proj_fm(wlqB, DC, LC, bigA, 1024, 2,
                                     (wb, stg, t1, t2, psP), False, None, 0,
                                     dst_sbuf=lqb, tag="lq")
                        kb.barrier(f"b.lq{half}")
                        emit_proj_fm(wlkB, LC, DC, lqb, 1024, 2,
                                     (wb, stg, t1, t2, psP), True, (cosFb, sinFb),
                                     toff, dst_dram=lk_scr[:, toff:toff + 1024],
                                     tag="lk")
                        kb.barrier(f"b.kv{half}")
            if not enabled("attn"):
                kb.check(); kb.emit(blk); return nc, kb

        with contextlib.ExitStack() as s34:
            maskS = s34.enter_context(nc.sbuf_tensor("maskS", [128, 16 * 512], BF16))
            kbuf = [s34.enter_context(nc.sbuf_tensor(nm(), [128, S], BF16))
                    for _ in range(2)]
            vbuf = [s34.enter_context(nc.sbuf_tensor(nm(), [128, S], BF16))
                    for _ in range(2)]
            qbuf = [s34.enter_context(nc.sbuf_tensor(nm(), [128, NO], BF16))
                    for _ in range(2)]
            Pb = [s34.enter_context(nc.sbuf_tensor(nm(), [128, 512], BF16))
                  for _ in range(6)]
            PMb = [s34.enter_context(nc.sbuf_tensor(nm(), [128, 512], BF16))
                   for _ in range(6)]
            recS = [s34.enter_context(nc.sbuf_tensor(nm(), [128, 512], F32))
                    for _ in range(2)]
            psS3 = [s34.enter_context(nc.psum_tensor(nm(), [128, 512], F32))
                    for _ in range(4)]
            psPV = [s34.enter_context(nc.psum_tensor(nm(), [128, 512], F32))
                    for _ in range(2)]
            psDen = [s34.enter_context(nc.psum_tensor(nm(), [128, 512], F32))
                     for _ in range(2)]
            dma("sync", maskS[:, :], maskI, "cst", tag="at.m")
            kb.barrier("b.at0")

            att = {"gs": 0, "hc": 0}

            def attention_phase(phase):
                k_src = k_scr if phase == 0 else lk_scr
                v_src = v_scr if phase == 0 else lv_scr
                for h in range(32):
                    hb = h % 2
                    kv = (h // 4) if phase == 0 else h
                    hw = [("aC", max(0, att["gs"] - 24))]
                    vX = None
                    for dst, src in (
                            (kbuf[hb][:, :], k_src[128 * kv:128 * kv + 128, :]),
                            (vbuf[hb][:, :],
                             v_src[:, kv * NT * 128:(kv + 1) * NT * 128]),
                            (qbuf[hb][:, :], q_scr[128 * h:128 * h + 128, :])):
                        vX = dma("sync", dst, src, f"sX{hb}", waits=hw, tag="at.ld")
                    for cc in range(2):
                        K = SLOT_KT[cc]
                        att["hc"] += 1
                        hc = att["hc"]
                        dp = (hc - 1) % 2
                        plist = []

                        def emit_pvden(kt, K=K, hb=hb, dp=dp, hc=hc):
                            src_b, vsrc = plist[kt]
                            st, sp = (kt == 0), (kt == K - 1)
                            w = [vsrc]
                            if st:
                                w.append(("aE", max(0, hc - 2)))
                            kb.op("tensor",
                                  lambda e, s=src_b,
                                  v=vbuf[hb][:, 128 * kt:128 * kt + 128],
                                  s1=st, s2=sp, d=dp:
                                  e.matmul(psPV[d][:, :], v, s, start=s1, stop=s2),
                                  waits=w, tag="at.pv")
                            kb.op("tensor",
                                  lambda e, s=src_b, s1=st, s2=sp, d=dp:
                                  e.matmul(psDen[d][:, :], onesb[:, :], s,
                                           start=s1, stop=s2),
                                  inc=("aC", 1), tag="at.den")

                        for kt in range(K):
                            att["gs"] += 1
                            gs = att["gs"]
                            bank = (gs - 1) % 4
                            pbank = (gs - 1) % 6
                            vS = kb.op("tensor",
                                       lambda e, ps=psS3[bank][:, :],
                                       l=kbuf[hb][:, 128 * kt:128 * kt + 128],
                                       r=qbuf[hb][:, 512 * cc:512 * cc + 512]:
                                       e.matmul(ps, l, r, start=True, stop=True),
                                       waits=[vX, ("aP", max(0, gs - 4))],
                                       inc=("aS", 1), tag="at.s")
                            bcol = 8 * cc + kt
                            vP = kb.op("scalar",
                                       lambda e, ps=psS3[bank][:, :],
                                       o=Pb[pbank][:, :], bc=bcol:
                                       e.activation(out=o, in_=ps, func=AF.Exp,
                                                    scale=SCALE,
                                                    bias=biasT[:, bc:bc + 1]),
                                       waits=[("aS", vS), ("aC", max(0, gs - 6))],
                                       inc=("aP", 1), tag="at.e")
                            masked = (cc == 0) or (kt >= 8)
                            if masked:
                                vMk = kb.op("vector",
                                            lambda e, i=Pb[pbank][:, :],
                                            o=PMb[pbank][:, :],
                                            m=maskS[:, 512 * kt:512 * kt + 512]:
                                            e.tensor_mul(o, i, m),
                                            waits=[("aP", vP),
                                                   ("aC", max(0, gs - 6))],
                                            inc=("aM", 1), tag="at.mk")
                                plist.append((PMb[pbank][:, :], ("aM", vMk)))
                            else:
                                plist.append((Pb[pbank][:, :], ("aP", vP)))
                            if kt >= 2:
                                emit_pvden(kt - 2)
                        emit_pvden(K - 2)
                        emit_pvden(K - 1)
                        vCend = kb.val("aC")

                        def epi(e, d=dp, hh=h, ccc=cc, rb=recS[hc % 2]):
                            e.reciprocal(rb[:, :], psDen[d][:, :])
                            e.drain()
                            return e.tensor_mul(
                                bigA[:, 1024 * hh + 512 * ccc:
                                     1024 * hh + 512 * ccc + 512],
                                psPV[d][:, :], rb[:, :])
                        kb.op("vector", epi, waits=[("aC", vCend)],
                              inc=("aE", 1), tag="at.ep")

            def oproj_phase(wsrc, tagp):
                with contextlib.ExitStack() as sc:
                    wb2 = [sc.enter_context(
                        nc.sbuf_tensor(nm(), [128, DC * 128], BF16))
                        for _ in range(3)]
                    ost = [sc.enter_context(nc.sbuf_tensor(nm(), [128, 512], F32))
                           for _ in range(4)]
                    psO = [psS3[0], psS3[1]]
                    vWs = {}
                    for j in range(2):
                        vWs[j] = dma("gpsimd", wb2[j % 3][:, :],
                                     wsrc[128 * j:128 * j + 128, :],
                                     f"oW{j % 3}", tag=tagp + ".wpre")
                    kb.barrier("b." + tagp + "0")
                    PEB = kb.val("sPE")
                    DVB = kb.val("sDV")
                    aprev = {}
                    for j in range(DC):
                        if j + 2 < DC:
                            vWs[j + 2] = dma(
                                "gpsimd", wb2[(j + 2) % 3][:, :],
                                wsrc[128 * (j + 2):128 * (j + 2) + 128, :],
                                f"oW{(j + 2) % 3}",
                                waits=[("sPE", PEB + max(0, j * 2))],
                                tag=tagp + ".w")
                        wv = vWs[j]
                        for tc in range(2):
                            g = j * 2 + tc
                            pb = g % 2
                            vM = None
                            for c in range(DC):
                                st, sp = (c == 0), (c == DC - 1)
                                w = ([wv, ("sDV", DVB + max(0, g - 1))]
                                     if st else [])
                                fn = (lambda e, l=wb2[j % 3][:, 128 * c:128 * c + 128],
                                      r=bigA[:, 1024 * c + 512 * tc:
                                             1024 * c + 512 * tc + 512],
                                      s1=st, s2=sp, ps=psO[pb][:, :]:
                                      e.matmul(ps, l, r, start=s1, stop=s2))
                                if sp:
                                    vM = kb.op("tensor", fn, waits=w,
                                               inc=("sPE", 1), tag=tagp + ".mm")
                                else:
                                    kb.op("tensor", fn, waits=w, tag=tagp + ".mm")
                            si = g % 4
                            ow = [("sPE", vM)]
                            if si in aprev:
                                ow.append(aprev[si])
                            vE = kb.op("vector",
                                       lambda e, ps=psO[pb][:, :], s=ost[si][:, :]:
                                       e.tensor_copy(s, ps),
                                       waits=ow, inc=("sDV", 1), tag=tagp + ".ev")
                            aprev[si] = dma(
                                "gpsimd",
                                outT[128 * j:128 * j + 128, 512 * tc:512 * tc + 512],
                                ost[si][:, :], f"sAc{si}", waits=[("sDV", vE)],
                                tag=tagp + ".acc", accum=True)

            attention_phase(0)
            if expose:
                kb.barrier("b.a1")
                for i in range(8):
                    dma("sync", at_dbg[:, 4096 * i:4096 * i + 4096],
                        bigA[:, 4096 * i:4096 * i + 4096], "sSt0", tag="dbg.at")
                kb.barrier("b.dbga")
            if enabled("wo"):
                oproj_phase(woB, "wo")
                kb.barrier("b.wo")
            if enabled("lat"):
                attention_phase(1)
                if expose:
                    kb.barrier("b.a2")
                    for i in range(8):
                        dma("sync", lat_dbg[:, 4096 * i:4096 * i + 4096],
                            bigA[:, 4096 * i:4096 * i + 4096], "sSt0", tag="dbg.lat")
                    kb.barrier("b.dbgl")
            if enabled("wlo"):
                oproj_phase(wloB, "lo")
                kb.barrier("b.end")

        kb.check()
        kb.emit(blk)
    return nc, kb


def kernel(L0_attn_hidden_states, L0_attn_w_q, L0_attn_w_k, L0_attn_w_v, L0_attn_w_o,
           L0_attn_w_lq, L0_attn_w_lk, L0_attn_w_lv, L0_attn_w_lo,
           L0_ffn_base_W, L0_ffn_base_b, L0_ffn_router_W, L0_ffn_router_b,
           L0_ffn_A, L0_ffn_B, L0_ffn_expert_map):
    global LAST_RES
    x = np.asarray(L0_attn_hidden_states, dtype=np.float32)

    shared = {
        "wqB": _wbands(np.asarray(L0_attn_w_q, np.float32), DC, 32, 128),
        "wkB": _wbands(np.asarray(L0_attn_w_k, np.float32), DC, 8, 128),
        "wvB": _wbands(np.asarray(L0_attn_w_v, np.float32), DC, 2, 512),
        "wlqB": _wbands(np.asarray(L0_attn_w_lq, np.float32), DC, 8, 128),
        "wlkB": _wbands(np.asarray(L0_attn_w_lk, np.float32), LC, 32, 128),
        "wlvB": _wbands(np.asarray(L0_attn_w_lv, np.float32), DC, 8, 512),
        "woB": _wbands(np.asarray(L0_attn_w_o, np.float32), DC, 32, 128),
        "wloB": _wbands(np.asarray(L0_attn_w_lo, np.float32), DC, 32, 128),
        "bWB": _wbands(np.ascontiguousarray(np.asarray(L0_ffn_base_W, np.float32).T),
                       DC, 32, 128),
        "rWB": _wbands(np.ascontiguousarray(np.asarray(L0_ffn_router_W, np.float32).T),
                       DC, 1, NR),
        "ATB": _wbands(np.ascontiguousarray(
            np.asarray(L0_ffn_A, np.float32).reshape(128, D).T), DC, 1, 128),
        "BTB": np.ascontiguousarray(
            np.asarray(L0_ffn_B, np.float32).transpose(0, 2, 1).reshape(128, D)),
        "bbI": np.ascontiguousarray(
            np.asarray(L0_ffn_base_b, np.float32).reshape(32, 128).T),
        "rbI": np.asarray(L0_ffn_router_b, np.float32).reshape(NR, 1),
        "emapI": np.asarray(L0_ffn_expert_map, np.int32).reshape(NR, 1),
        "iotaI": np.tile(np.arange(NE, dtype=np.float32)[None, :], (NR, 1)),
        "p8I": _p8pad(),
        "onesI": np.ones((128, 128), dtype=BF),
        "eyeI": np.eye(128, dtype=np.float32),
    }
    percore = []
    for c in range(8):
        b, p = c // 2, c % 2
        own, cosO, sinO, cosF, sinF, biasA, masks = _host_consts(p)
        xt = np.ascontiguousarray(x[b].T)
        xo = np.ascontiguousarray(xt[:, own])
        m = dict(shared)
        m.update({
            "xoH": _xmajor(xo, NO),
            "xbH": _xmajor(xt, S),
            "cosO": cosO.astype(BF), "sinO": sinO.astype(BF),
            "cosF": np.ascontiguousarray(cosF).astype(BF),
            "sinF": np.ascontiguousarray(sinF).astype(BF),
            "biasI": biasA, "maskI": masks,
        })
        percore.append(m)

    nc, kb = build_program(UPTO, DEV_EXPOSE, SUB2A)
    res = run_bass_kernel_spmd(nc, percore, core_ids=list(range(8)))
    LAST_RES = res

    out = np.empty((B, S, D), dtype=np.float32)
    for c in range(8):
        b, p = c // 2, c % 2
        own = np.concatenate([np.arange(128 * t, 128 * t + 128)
                              for t in OWN_TILES[p]])
        out[b, own, :] = res.results[c]["outT"].T
    return out


# version 09 (final)
# speedup vs baseline: 1.1729x; 1.0030x over previous
import contextlib
from collections import defaultdict

import numpy as np
import ml_dtypes
import concourse.bass as bass
import concourse.mybir as mybir
from concourse.bass_utils import run_bass_kernel_spmd

F32 = mybir.dt.float32
BF16 = mybir.dt.bfloat16
I32 = mybir.dt.int32
AF = mybir.ActivationFunctionType
ALU = mybir.AluOpType
AXX = mybir.AxisListType.X
BF = ml_dtypes.bfloat16

B, S, D, HD = 4, 2048, 4096, 128
NH, NKV, LAT = 32, 8, 1024
DC = D // 128
LC = LAT // 128
NT = S // 128
NO = 1024
NR, NE = 15, 8
SCALE = 1.0 / float(np.sqrt(HD))
SHIFT = -12.0
NEG = -3.0e38
EPS = 1e-5
THETA = 10000.0
OWN_TILES = [[0, 1, 2, 3, 12, 13, 14, 15], [4, 5, 6, 7, 8, 9, 10, 11]]
OWN_BASE = [[0, 12], [4, 8]]
SLOT_KT = [8, 16]

ENGS = ("sync", "gpsimd", "tensor", "vector", "scalar")

UPTO = "full"
SUB2A = "all"
DEV_EXPOSE = False
LAST_RES = None


class KB:
    def __init__(self):
        self.ops = {e: [] for e in ENGS}
        self.cnt = defaultdict(int)
        self.sem = {}
        self.last_wait = {}

    def val(self, name):
        return self.cnt[name]

    def op(self, eng, fn, waits=(), inc=None, tag=""):
        w = tuple((s, v) for (s, v) in waits if v > 0)
        v = None
        if inc is not None:
            self.cnt[inc[0]] += inc[1]
            v = self.cnt[inc[0]]
        self.ops[eng].append({"fn": fn, "waits": w, "inc": inc, "tag": tag})
        return v

    def barrier(self, tag="bar"):
        snap = [(s, v) for s, v in self.cnt.items() if v > 0]
        for e in ENGS:
            self.op(e, None, waits=snap, tag=tag)

    def check(self):
        ptr = {e: 0 for e in ENGS}
        semv = defaultdict(int)
        total = sum(len(v) for v in self.ops.values())
        done = 0
        progress = True
        while progress:
            progress = False
            for e in ENGS:
                while ptr[e] < len(self.ops[e]):
                    o = self.ops[e][ptr[e]]
                    if all(semv[s] >= v for s, v in o["waits"]):
                        if o["inc"] is not None:
                            semv[o["inc"][0]] += o["inc"][1]
                        ptr[e] += 1
                        done += 1
                        progress = True
                    else:
                        break
        if done < total:
            stuck = {}
            for e in ENGS:
                if ptr[e] < len(self.ops[e]):
                    o = self.ops[e][ptr[e]]
                    blocked = [(s, v, semv[s]) for s, v in o["waits"] if semv[s] < v]
                    stuck[e] = (ptr[e], o["tag"], blocked)
            raise RuntimeError(f"DEADLOCK: {stuck}")
        for e in ENGS:
            for o in self.ops[e]:
                for s, v in o["waits"]:
                    assert v <= self.cnt[s], \
                        f"wait {s}>={v} but total {self.cnt[s]} ({o['tag']})"

    def emit(self, block):
        def mk(ename):
            def body(eng):
                for o in self.ops[ename]:
                    for s, v in o["waits"]:
                        k = (ename, s)
                        if self.last_wait.get(k, 0) >= v:
                            continue
                        self.last_wait[k] = v
                        eng.wait_ge(self.sem[s], v)
                    if o["fn"] is not None:
                        r = o["fn"](eng)
                        if o["inc"] is not None:
                            s, by = o["inc"]
                            r.then_inc(self.sem[s], by)
            return body
        for ename, meth in (("sync", block.sync), ("gpsimd", block.gpsimd),
                            ("tensor", block.tensor), ("vector", block.vector),
                            ("scalar", block.scalar)):
            if self.ops[ename]:
                meth(mk(ename))


def _rope_tables():
    inv = THETA ** (-np.arange(0, HD, 2, dtype=np.float64) / HD)
    t = np.arange(S, dtype=np.float64)
    fr = np.outer(t, inv)
    emb = np.concatenate([fr, fr], axis=1)
    cos = np.cos(emb).T.astype(np.float32)
    sin = np.sin(emb).T.astype(np.float32)
    ssgn = sin.copy()
    ssgn[0:64, :] = -ssgn[0:64, :]
    return cos, ssgn


def _host_consts(parity):
    cos, ssgn = _rope_tables()
    own = np.concatenate([np.arange(128 * t, 128 * t + 128) for t in OWN_TILES[parity]])
    cosO = np.ascontiguousarray(cos[:, own])
    sinO = np.ascontiguousarray(ssgn[:, own])
    bias = np.full((128, 24), NEG, dtype=np.float32)
    masks = np.ones((128, 16, 512), dtype=np.float32)
    for cc in range(2):
        base = OWN_BASE[parity][cc]
        for kt in range(SLOT_KT[cc]):
            if kt <= base + 3:
                bias[:, 8 * cc + kt] = SHIFT
        for kt in (range(8) if cc == 0 else range(8, 16)):
            if base <= kt <= base + 3:
                j = kt - base
                p = np.arange(128)[:, None]
                c = np.arange(512)[None, :]
                masks[:, kt, :] = (c >= 128 * j + p).astype(np.float32)
    masks = np.ascontiguousarray(masks.reshape(128, 16 * 512)).astype(BF)
    return own, cosO, sinO, cos, ssgn, bias, masks


def _p8pad():
    p = np.zeros((128, 128), dtype=np.float32)
    for e in range(NE):
        p[e, e * 16:(e + 1) * 16] = 2.0
    return p


def _wbands(w, nch, nj, ncol):
    return np.ascontiguousarray(
        w.reshape(nch, 128, nj, ncol).transpose(2, 1, 0, 3).reshape(nj * 128, nch * ncol))


def _xmajor(xt, ncols):
    return np.ascontiguousarray(
        xt.reshape(32, 128, ncols).transpose(1, 0, 2).reshape(128, 32 * ncols))


def build_program(upto=UPTO, expose=DEV_EXPOSE, sub2a=None):
    if sub2a is None:
        sub2a = SUB2A
    nc = bass.Bass("TRN2", target_bir_lowering=False, debug=False)
    kb = KB()
    SEM = kb.sem
    phases = ["ln", "ffn", "q", "kv", "attn", "wo", "lat", "wlo", "full"]
    plim = phases.index(upto)

    def enabled(ph):
        return phases.index(ph) <= plim

    def din(name, shape, dt=F32):
        return nc.dram_tensor(name, shape, dt, kind="ExternalInput").ap()

    def dscr(name, shape, dt=BF16):
        kind = "ExternalOutput" if expose else "Internal"
        return nc.dram_tensor(name, shape, dt, kind=kind).ap()

    xoH = din("xoH", [128, DC * NO])
    xbH = din("xbH", [128, DC * S])
    wqB = din("wqB", [32 * 128, DC * 128])
    wkB = din("wkB", [8 * 128, DC * 128])
    wvB = din("wvB", [2 * 128, DC * 512])
    wlqB = din("wlqB", [8 * 128, DC * 128])
    wlkB = din("wlkB", [32 * 128, LC * 128])
    wlvB = din("wlvB", [8 * 128, DC * 512])
    woB = din("woB", [32 * 128, DC * 128])
    wloB = din("wloB", [32 * 128, DC * 128])
    bWB = din("bWB", [32 * 128, DC * 128])
    rWB = din("rWB", [128, DC * NR])
    ATB = din("ATB", [128, DC * 128])
    BTB = din("BTB", [128, D])
    bbI = din("bbI", [128, 32])
    rbI = din("rbI", [NR, 1])
    emapI = din("emapI", [NR, 1], I32)
    iotaI = din("iotaI", [NR, NE])
    p8I = din("p8I", [128, 128])
    onesI = din("onesI", [128, 128], BF16)
    eyeI = din("eyeI", [128, 128])
    cosO_I = din("cosO", [128, NO], BF16)
    sinO_I = din("sinO", [128, NO], BF16)
    cosF_I = din("cosF", [128, S], BF16)
    sinF_I = din("sinF", [128, S], BF16)
    biasI = din("biasI", [128, 24])
    maskI = din("maskI", [128, 16 * 512], BF16)

    outT = nc.dram_tensor("outT", [D, NO], F32, kind="ExternalOutput").ap()
    if expose:
        xn_dbg = dscr("xn_dbg", [128, DC * 1024])
        hw_dbg = dscr("hw_dbg", [128, NO])
        lg_dbg = dscr("lg_dbg", [NR, NO], F32)
        mp_dbg = dscr("mp_dbg", [NR, NO], F32)
        at_dbg = dscr("at_dbg", [128, DC * 1024])
        lat_dbg = dscr("lat_dbg", [128, DC * 1024])
    q_scr = dscr("q_scr", [D, NO])
    k_scr = dscr("k_scr", [LAT, S])
    lk_scr = dscr("lk_scr", [D, S])
    v_scr = dscr("v_scr", [128, NKV * NT * 128])
    lv_scr = dscr("lv_scr", [128, NH * NT * 128])

    big = contextlib.ExitStack()
    with big:
        for n in ["cst", "sW0", "sW1", "sX0", "sX1",
                  "sSt0", "sSt1", "sSt2", "sSt3",
                  "sAc0", "sAc1", "sAc2", "sAc3",
                  "oW0", "oW1", "oW2", "sPE", "sDV", "sAC",
                  "aS", "aP", "aM", "aC", "aE"]:
            SEM[n] = big.enter_context(nc.semaphore(n))

        onesb = big.enter_context(nc.sbuf_tensor("onesb", [128, 128], BF16))
        eyeS = big.enter_context(nc.sbuf_tensor("eyeS", [128, 128], F32))
        biasT = big.enter_context(nc.sbuf_tensor("biasT", [128, 24], F32))
        bbS = big.enter_context(nc.sbuf_tensor("bbS", [128, 32], F32))
        epsT = big.enter_context(nc.sbuf_tensor("epsT", [128, 1], F32))
        hwT = big.enter_context(nc.sbuf_tensor("hwT", [128, NO], BF16))
        bigA = big.enter_context(nc.sbuf_tensor("bigA", [128, DC * 1024], BF16))

        blk = big.enter_context(nc.Block())

        _ctr = [0]

        def nm():
            _ctr[0] += 1
            return f"t{_ctr[0]}"

        def dma(eng, dst, src, sem, waits=(), tag="", accum=False):
            if accum:
                fn = lambda e, d=dst, s=src: e.dma_start(out=d, in_=s, accum_op=ALU.add)
            else:
                fn = lambda e, d=dst, s=src: e.dma_start(out=d, in_=s)
            v = kb.op(eng, fn, waits=waits, inc=(sem, 16), tag=tag)
            return (sem, v)

        for dst, src in ((onesb[:, :], onesI), (eyeS[:, :], eyeI), (biasT[:, :], biasI),
                         (bbS[:, :], bbI)):
            dma("sync", dst, src, "cst", tag="cst")
        kb.op("vector", lambda e: e.memset(epsT[:, :], EPS), tag="eps")
        kb.op("vector", lambda e: e.drain(), tag="eps.dr")
        kb.op("vector", lambda e: e.tensor_copy(epsT[0:1, 0:1], epsT[0:1, 0:1]),
              inc=("sDV", 1), tag="eps.ms")
        kb.barrier("b.cst")

        def alloc_ln(sc):
            CW = 256
            b = {"CW": CW}
            b["xbf"] = [sc.enter_context(nc.sbuf_tensor(nm(), [128, DC * CW], BF16))
                        for _ in range(2)]
            b["sqr"] = [sc.enter_context(nc.sbuf_tensor(nm(), [128, CW], BF16))
                        for _ in range(4)]
            for k in ["negm", "qm", "mm2", "varb", "sdb"]:
                b[k] = sc.enter_context(nc.sbuf_tensor(nm(), [128, CW], F32))
            b["rstd"] = [sc.enter_context(nc.sbuf_tensor(nm(), [128, CW], F32))
                         for _ in range(2)]
            b["bb"] = [sc.enter_context(nc.sbuf_tensor(nm(), [128, CW], F32))
                       for _ in range(2)]
            b["tmp"] = sc.enter_context(nc.sbuf_tensor(nm(), [128, CW], F32))
            b["psS"] = [sc.enter_context(nc.psum_tensor(nm(), [128, CW], F32))]
            b["psQ"] = [sc.enter_context(nc.psum_tensor(nm(), [128, CW], F32))]
            return b

        def emit_ln(b, src_cols, ntc, dst, dst_off):
            CW = b["CW"]
            applies = {}
            stats_done = {}
            sqrts = {}
            for i in range(ntc):
                xb2 = b["xbf"][i % 2]
                psS, psQ = b["psS"][0], b["psQ"][0]
                rstd, bbb = b["rstd"][i % 2], b["bb"][i % 2]
                wv = dma("gpsimd", xb2[:, :].rearrange("p (c n) -> p c n", n=CW),
                         src_cols(i), f"sW{i % 2}",
                         waits=[("sDV", applies.get(i - 2, 0))], tag="ln.x")
                pe0 = kb.val("sPE")
                for c in range(DC):
                    vA = kb.op("scalar",
                               lambda e, o=b["sqr"][c % 4][:, :],
                               i_=xb2[:, CW * c:CW * c + CW]:
                               e.activation(out=o, in_=i_, func=AF.Square),
                               waits=[wv, ("sPE", pe0 + (c - 3))],
                               inc=("sAC", 1), tag="ln.sq")
                    kb.op("tensor",
                          lambda e, i_=xb2[:, CW * c:CW * c + CW],
                          st=(c == 0), sp=(c == DC - 1), ps=psS:
                          e.matmul(ps[:, :], onesb[:, :], i_, start=st, stop=sp),
                          waits=([wv, ("sDV", stats_done.get(i - 1, 0))]
                                 if c == 0 else []),
                          tag="ln.s")
                    kb.op("tensor",
                          lambda e, i_=b["sqr"][c % 4][:, :],
                          st=(c == 0), sp=(c == DC - 1), ps=psQ:
                          e.matmul(ps[:, :], onesb[:, :], i_, start=st, stop=sp),
                          waits=[("sAC", vA)], inc=("sPE", 1), tag="ln.q")
                pe_done = kb.val("sPE")

                def stats(e, b=b, psS=psS, psQ=psQ):
                    e.tensor_scalar_mul(b["negm"][:, :], psS[:, :], -1.0 / D)
                    e.tensor_scalar_mul(b["qm"][:, :], psQ[:, :], 1.0 / D)
                    e.drain()
                    e.tensor_mul(b["mm2"][:, :], b["negm"][:, :], b["negm"][:, :])
                    e.drain()
                    return e.tensor_sub(b["varb"][:, :], b["qm"][:, :], b["mm2"][:, :])
                vD = kb.op("vector", stats,
                           waits=[("sPE", pe_done), ("sAC", sqrts.get(i - 1, 0))],
                           inc=("sDV", 1), tag="ln.st")
                stats_done[i] = vD
                vA2 = kb.op("scalar",
                            lambda e, b=b: e.activation(
                                out=b["sdb"][:, :], in_=b["varb"][:, :],
                                func=AF.Sqrt, bias=epsT[:, 0:1]),
                            waits=[("sDV", vD)], inc=("sAC", 1), tag="ln.sqrt")
                sqrts[i] = vA2

                def fin(e, b=b, rstd=rstd, bbb=bbb):
                    e.reciprocal(rstd[:, :], b["sdb"][:, :])
                    e.drain()
                    return e.tensor_mul(bbb[:, :], b["negm"][:, :], rstd[:, :])
                vD2 = kb.op("vector", fin, waits=[("sAC", vA2)],
                            inc=("sDV", 1), tag="ln.fin")

                def apply(e, b=b, ii=i, xb2=xb2, rstd=rstd, bbb=bbb):
                    for c in range(DC):
                        e.tensor_mul(b["tmp"][:, :], xb2[:, CW * c:CW * c + CW],
                                     rstd[:, :])
                        e.drain()
                        e.tensor_add(dst[:, 1024 * c + dst_off + CW * ii:
                                         1024 * c + dst_off + CW * ii + CW],
                                     b["tmp"][:, :], bbb[:, :])
                        e.drain()
                    return e.tensor_copy(b["tmp"][0:1, 0:1], b["tmp"][0:1, 0:1])
                applies[i] = kb.op("vector", apply, waits=[("sDV", vD2)],
                                   inc=("sDV", 1), tag="ln.ap")

        def emit_proj_fm(wsrc, nch, njs, rhs, rhs_stride, ntc, bufs, rope, cos_sin,
                         cos_off, dst_dram=None, dst_sbuf=None, tag="pj"):
            wb, stg, t1, t2, psP = bufs
            PEB = kb.val("sPE")
            DVB = kb.val("sDV")
            prev_store = stg_prev
            for ji in range(njs):
                wv = dma("gpsimd", wb[ji % 2][:, 0:nch * 128],
                         wsrc[128 * ji:128 * ji + 128, :],
                         f"sW{ji % 2}", waits=[("sPE", PEB + max(0, (ji - 1) * ntc))],
                         tag=tag + ".w")
                for tc in range(ntc):
                    g = ji * ntc + tc
                    pb = g % 3
                    vM = None
                    for c in range(nch):
                        st, sp = (c == 0), (c == nch - 1)
                        w = [wv, ("sDV", DVB + max(0, g - 2))] if st else []
                        fn = (lambda e, ps=psP[pb][:, :],
                              l=wb[ji % 2][:, 128 * c:128 * c + 128],
                              r=rhs[:, rhs_stride * c + 512 * tc:
                                    rhs_stride * c + 512 * tc + 512],
                              s1=st, s2=sp: e.matmul(ps, l, r, start=s1, stop=s2))
                        if sp:
                            vM = kb.op("tensor", fn, waits=w, inc=("sPE", 1),
                                       tag=f"{tag}.mm")
                        else:
                            kb.op("tensor", fn, waits=w, tag=f"{tag}.mm")
                    si = g % 4
                    stw = [("sPE", vM)]
                    if dst_dram is not None and si in prev_store:
                        stw.append(prev_store[si])
                    if rope:
                        cosb, sinb = cos_sin
                        co = cos_off + 512 * tc

                        def evac(e, ps=psP[pb], s=stg[si], co=co):
                            e.drain()
                            e.tensor_mul(t1[:, :], ps[:, :], cosb[:, co:co + 512])
                            e.tensor_mul(t2[0:64, :], ps[64:128, :],
                                         sinb[0:64, co:co + 512])
                            e.tensor_mul(t2[64:128, :], ps[0:64, :],
                                         sinb[64:128, co:co + 512])
                            e.drain()
                            return e.tensor_add(s[:, :], t1[:, :], t2[:, :])
                    elif dst_dram is not None:
                        def evac(e, ps=psP[pb], s=stg[si]):
                            return e.tensor_copy(s[:, :], ps[:, :])
                    else:
                        def evac(e, ps=psP[pb], jj=ji, tcc=tc):
                            return e.tensor_copy(
                                dst_sbuf[:, 1024 * jj + 512 * tcc:
                                         1024 * jj + 512 * tcc + 512], ps[:, :])
                    vE = kb.op("vector", evac, waits=stw, inc=("sDV", 1),
                               tag=f"{tag}.ev")
                    if dst_dram is not None:
                        prev_store[si] = dma(
                            "sync",
                            dst_dram[128 * ji:128 * ji + 128, 512 * tc:512 * tc + 512],
                            stg[si][:, :], f"sSt{si}", waits=[("sDV", vE)],
                            tag=tag + ".st")

        with contextlib.ExitStack() as sc:
            lb = alloc_ln(sc)
            xo3 = xoH.rearrange("p (c n) -> p c n", n=NO)
            emit_ln(lb, lambda tc: xo3[:, :, 256 * tc:256 * tc + 256], NO // 256,
                    bigA, 0)
            kb.barrier("b.ln")
            if expose:
                for i in range(8):
                    dma("sync", xn_dbg[:, 4096 * i:4096 * i + 4096],
                        bigA[:, 4096 * i:4096 * i + 4096], "sSt0", tag="dbg.xn")
                kb.barrier("b.dbg1")
        if not enabled("ffn"):
            kb.check(); kb.emit(blk); return nc, kb

        with contextlib.ExitStack() as sc:
            E = lambda n, sh, dt=F32: sc.enter_context(nc.sbuf_tensor(n, sh, dt))
            rWTb = E("rWTb", [128, DC * NR], BF16)
            ATb = E("ATb", [128, DC * 128], BF16)
            mpS = E("mpS", [128, NO])
            wsb = E("wsb", [128, 512])
            wexpS = E("wexpS", [128, 512])
            EmapP = E("EmapP", [128, NE])
            p8S = E("p8S", [128, 128])
            iotaS = E("iotaS", [NR, NE])
            emapS = E("emapS", [NR, 1], I32)
            emapF = E("emapF", [NR, 1])
            rbS = E("rbS", [NR, 1])
            lgf = E("lgf", [NR, NO])
            ltok = E("ltok", [128, NR])
            eo = E("eo", [128, NR])
            nmax = E("nmax", [128, 1])
            sume = E("sume", [128, 1])
            recp = E("recp", [128, 1])
            beat = E("beat", [128, NR])
            eqc = E("eqc", [128, NR])
            tot = E("tot", [128, NR])
            mps = E("mps", [128, NR])
            junk = E("junk", [128, NR])
            psR = sc.enter_context(nc.psum_tensor("psR", [128, 512], F32))
            psT = sc.enter_context(nc.psum_tensor("psT", [128, 128], F32))
            psW = sc.enter_context(nc.psum_tensor("psW", [128, 512], F32))
            psH = sc.enter_context(nc.psum_tensor("psH", [128, 512], F32))

            vW1 = dma("gpsimd", rWTb[:, :], rWB, "sW0", tag="rt.w")
            vW2 = dma("gpsimd", ATb[:, :], ATB, "sW1", tag="rt.at")
            for dst, src in ((p8S[:, :], p8I), (iotaS[:, :], iotaI),
                             (emapS[:, :], emapI), (rbS[:, :], rbI)):
                dma("sync", dst, src, "cst", tag="rt.c")
            CC = kb.val("cst")

            def prep(e):
                e.tensor_copy(emapF[:, :], emapS[:, :])
                e.memset(EmapP[:, :], 0.0)
                e.drain()
                e.tensor_single_scalar(EmapP[0:NR, :], iotaS[:, :], emapF[:, 0:1],
                                       ALU.is_equal)
                e.drain()
                return e.tensor_copy(EmapP[0:1, 0:1], EmapP[0:1, 0:1])
            vPrep = kb.op("vector", prep, waits=[("cst", CC)], inc=("sDV", 1),
                          tag="rt.prep")

            for tc in range(2):
                vP = None
                for c in range(DC):
                    st, sp = (c == 0), (c == DC - 1)
                    w = [vW1, ("sDV", kb.val("sDV"))] if st else []
                    fn = (lambda e, l=rWTb[:, NR * c:NR * c + NR],
                          r=bigA[:, 1024 * c + 512 * tc:1024 * c + 512 * tc + 512],
                          s1=st, s2=sp:
                          e.matmul(psR[0:NR, :], l, r, start=s1, stop=s2))
                    if sp:
                        vP = kb.op("tensor", fn, waits=w, inc=("sPE", 1), tag="rt.mm")
                    else:
                        kb.op("tensor", fn, waits=w, tag="rt.mm")
                kb.op("vector",
                      lambda e, d=lgf[:, 512 * tc:512 * tc + 512]:
                      e.tensor_scalar_add(d, psR[0:NR, 0:512], rbS[:, 0:1]),
                      waits=[("sPE", vP), ("cst", CC)], inc=("sDV", 1), tag="rt.lg")
            LGD = kb.val("sDV")
            if expose and sub2a in ("lg", "tk"):
                dma("sync", lg_dbg, lgf[0:NR, :], "sSt0",
                    waits=[("sDV", LGD)], tag="dbg.lg")
            if sub2a == "lg":
                kb.barrier("b.rt")
                kb.check(); kb.emit(blk); return nc, kb

            prev_vAe = 0
            for tt in range(NO // 128):
                vT = kb.op("tensor",
                           lambda e, i=lgf[0:NR, 128 * tt:128 * tt + 128]:
                           e.transpose(psT[:, 0:NR], i, eyeS[0:NR, 0:NR]),
                           waits=[("sDV", LGD), ("sDV", kb.val("sDV"))],
                           inc=("sPE", 1), tag="tk.tr")

                def tokstat(e):
                    e.tensor_copy(ltok[:, :], psT[:, 0:NR])
                    e.drain()
                    e.reduce_max(nmax[:, :], ltok[:, :], axis=AXX, negate=True)
                    e.drain()
                    return e.tensor_copy(nmax[0:1, 0:1], nmax[0:1, 0:1])
                vDm = kb.op("vector", tokstat,
                            waits=[("sPE", vT), ("sAC", prev_vAe)],
                            inc=("sDV", 1), tag="tk.st")
                vAe = kb.op("scalar",
                            lambda e: e.activation(out=eo[:, :], in_=ltok[:, :],
                                                   func=AF.Exp, bias=nmax[:, 0:1],
                                                   accum_out=sume[:, :]),
                            waits=[("sDV", vDm)], inc=("sAC", 1), tag="tk.exp")
                prev_vAe = vAe

                def select(e):
                    e.reciprocal(recp[:, :], sume[:, :])
                    e.memset(eqc[:, :], 0.0)
                    e.drain()
                    for i in range(NR):
                        e.tensor_scalar(junk[:, :], eo[:, :], eo[:, i:i + 1], None,
                                        ALU.is_gt, op1=ALU.add,
                                        accum_out=beat[:, i:i + 1])
                        e.drain()
                    for i in range(1, NR):
                        e.tensor_scalar(junk[:, 0:i], eo[:, 0:i], eo[:, i:i + 1], None,
                                        ALU.is_equal, op1=ALU.add,
                                        accum_out=eqc[:, i:i + 1])
                        e.drain()
                    e.tensor_add(tot[:, :], beat[:, :], eqc[:, :])
                    e.drain()
                    e.tensor_single_scalar(tot[:, :], tot[:, :], 7.5, ALU.is_lt)
                    e.drain()
                    e.tensor_mul(mps[:, :], tot[:, :], eo[:, :])
                    e.drain()
                    e.tensor_scalar_mul(mps[:, :], mps[:, :], recp[:, 0:1])
                    e.drain()
                    return e.tensor_copy(mps[0:1, 0:1], mps[0:1, 0:1])
                vDs = kb.op("vector", select, waits=[("sAC", vAe)],
                            inc=("sDV", 1), tag="tk.sel")
                vT2 = kb.op("tensor",
                            lambda e: e.transpose(psT[0:NR, 0:128], mps[:, :],
                                                  eyeS[:, :]),
                            waits=[("sDV", vDs)], inc=("sPE", 1), tag="tk.tr2")
                kb.op("vector",
                      lambda e, d=mpS[0:NR, 128 * tt:128 * tt + 128]:
                      e.tensor_copy(d, psT[0:NR, 0:128]),
                      waits=[("sPE", vT2)], inc=("sDV", 1), tag="tk.cp")

            if expose and sub2a == "tk":
                dma("sync", mp_dbg, mpS[0:NR, :], "sSt0",
                    waits=[("sDV", kb.val("sDV"))], tag="dbg.mp")
            if sub2a == "tk":
                kb.barrier("b.rt")
                kb.check(); kb.emit(blk); return nc, kb

            for tc in range(2):
                vP1 = kb.op("tensor",
                            lambda e, r=mpS[0:NR, 512 * tc:512 * tc + 512]:
                            e.matmul(psW[0:NE, 0:512], EmapP[0:NR, 0:NE], r,
                                     start=True, stop=True),
                            waits=[("sDV", kb.val("sDV")), ("sDV", vPrep)],
                            inc=("sPE", 1), tag="lw.w")
                vD1 = kb.op("vector",
                            lambda e: e.tensor_copy(wsb[0:NE, :], psW[0:NE, 0:512]),
                            waits=[("sPE", vP1)], inc=("sDV", 1), tag="lw.cp")
                vP2 = kb.op("tensor",
                            lambda e: e.matmul(psW[:, :], p8S[0:NE, :], wsb[0:NE, :],
                                               start=True, stop=True),
                            waits=[("sDV", vD1), ("cst", CC)],
                            inc=("sPE", 1), tag="lw.ex")
                vD2 = kb.op("vector",
                            lambda e: e.tensor_copy(wexpS[:, :], psW[:, :]),
                            waits=[("sPE", vP2)], inc=("sDV", 1), tag="lw.excp")
                vPh = None
                for c in range(DC):
                    st, sp = (c == 0), (c == DC - 1)
                    w = [vW2, ("sDV", vD2)] if st else []
                    fn = (lambda e, l=ATb[:, 128 * c:128 * c + 128],
                          r=bigA[:, 1024 * c + 512 * tc:1024 * c + 512 * tc + 512],
                          s1=st, s2=sp:
                          e.matmul(psH[:, :], l, r, start=s1, stop=s2))
                    if sp:
                        vPh = kb.op("tensor", fn, waits=w, inc=("sPE", 1), tag="lw.h")
                    else:
                        kb.op("tensor", fn, waits=w, tag="lw.h")
                kb.op("vector",
                      lambda e, d=hwT[:, 512 * tc:512 * tc + 512]:
                      e.tensor_mul(d, psH[:, :], wexpS[:, :]),
                      waits=[("sPE", vPh)], inc=("sDV", 1), tag="lw.hw")
            if expose:
                dma("sync", hw_dbg[:, :], hwT[:, :], "sSt0",
                    waits=[("sDV", kb.val("sDV"))], tag="dbg.hw")
            kb.barrier("b.rt")
        if not enabled("q"):
            kb.check(); kb.emit(blk); return nc, kb

        with contextlib.ExitStack() as scp:
            stg_prev = {}
            wb = [scp.enter_context(nc.sbuf_tensor(nm(), [128, DC * 128], BF16))
                  for _ in range(2)]
            stg = [scp.enter_context(nc.sbuf_tensor(nm(), [128, 512], BF16))
                   for _ in range(4)]
            t1 = scp.enter_context(nc.sbuf_tensor(nm(), [128, 512], F32))
            t2 = scp.enter_context(nc.sbuf_tensor(nm(), [128, 512], F32))
            psP = [scp.enter_context(nc.psum_tensor(nm(), [128, 512], F32))
                   for _ in range(3)]
            scq = contextlib.ExitStack()
            cosOb = scq.enter_context(nc.sbuf_tensor("cosOb", [128, NO], BF16))
            sinOb = scq.enter_context(nc.sbuf_tensor("sinOb", [128, NO], BF16))
            for dst, src in ((cosOb[:, :], cosO_I), (sinOb[:, :], sinO_I)):
                dma("sync", dst, src, "cst", tag="ro.c")
            kb.barrier("b.q0")

            emit_proj_fm(wqB, DC, 32, bigA, 1024, 2, (wb, stg, t1, t2, psP),
                         True, (cosOb, sinOb), 0, dst_dram=q_scr, tag="q")
            kb.barrier("b.q")
            scq.close()

            with contextlib.ExitStack() as sc:
                BTIb = sc.enter_context(nc.sbuf_tensor("BTIb", [128, D], BF16))
                xot = [sc.enter_context(nc.sbuf_tensor(nm(), [128, 512], F32))
                       for _ in range(4)]
                ost = [sc.enter_context(nc.sbuf_tensor(nm(), [128, 512], F32))
                       for _ in range(4)]
                vWB = dma("gpsimd", BTIb[:, :], BTB, "sW0", tag="fo.bt")
                PEB = kb.val("sPE")
                DVB = kb.val("sDV")
                prev_st = {}
                for j in range(DC):
                    wv = dma("gpsimd", wb[j % 2][:, :],
                             bWB[128 * j:128 * j + 128, :],
                             f"sW{j % 2}", waits=[("sPE", PEB + max(0, (j - 1) * 2))],
                             tag="fo.w")
                    for tc in range(2):
                        g = j * 2 + tc
                        pb = g % 3
                        vX = dma("sync", xot[g % 4][:, :],
                                 xoH[:, j * NO + 512 * tc:j * NO + 512 * tc + 512],
                                 f"sX{g % 2}", waits=[("sDV", DVB + max(0, g - 3))],
                                 tag="fo.x")
                        for c in range(DC):
                            st = (c == 0)
                            w = ([wv, ("sDV", DVB + max(0, g - 2))]
                                 if st else [])
                            fn = (lambda e, l=wb[j % 2][:, 128 * c:128 * c + 128],
                                  r=bigA[:, 1024 * c + 512 * tc:
                                         1024 * c + 512 * tc + 512],
                                  s1=st, ps=psP[pb][:, :]:
                                  e.matmul(ps, l, r, start=s1, stop=False))
                            kb.op("tensor", fn, waits=w, tag="fo.mm")
                        vM = kb.op("tensor",
                                   lambda e, l=BTIb[:, 128 * j:128 * j + 128],
                                   r=hwT[:, 512 * tc:512 * tc + 512],
                                   ps=psP[pb][:, :]:
                                   e.matmul(ps, l, r, start=False, stop=True),
                                   waits=[vWB], inc=("sPE", 1), tag="fo.lb")
                        si = g % 4

                        def evac(e, ps=psP[pb], s=ost[si], jj=j, xo=xot[g % 4]):
                            e.drain()
                            e.tensor_scalar_add(t1[:, :], ps[:, :], bbS[:, jj:jj + 1])
                            e.drain()
                            return e.tensor_add(s[:, :], t1[:, :], xo[:, :])
                        fw = [("sPE", vM), vX]
                        if si in prev_st:
                            fw.append(prev_st[si])
                        vE = kb.op("vector", evac, waits=fw,
                                   inc=("sDV", 1), tag="fo.ev")
                        prev_st[si] = dma(
                            "sync",
                            outT[128 * j:128 * j + 128, 512 * tc:512 * tc + 512],
                            ost[si][:, :], f"sSt{si}", waits=[("sDV", vE)],
                            tag="fo.st")
                kb.barrier("b.fo")
            if not enabled("kv"):
                kb.check(); kb.emit(blk); return nc, kb

            with contextlib.ExitStack() as sc:
                cosFb = sc.enter_context(nc.sbuf_tensor("cosFb", [128, S], BF16))
                sinFb = sc.enter_context(nc.sbuf_tensor("sinFb", [128, S], BF16))
                for dst, src in ((cosFb[:, :], cosF_I), (sinFb[:, :], sinF_I)):
                    dma("sync", dst, src, "cst", tag="kv.c")
                xb3 = xbH.rearrange("p (c n) -> p c n", n=S)
                wvb = [sc.enter_context(nc.sbuf_tensor(nm(), [128, DC * 512], BF16))
                       for _ in range(2)]
                psV = [sc.enter_context(nc.psum_tensor(nm(), [128, 512], F32))
                       for _ in range(2)]
                vpre = {}
                vlast_pe = [0, 0]
                for half in range(2):
                    toff = 1024 * half
                    for jc in range(2):
                        wsrc0 = wvB
                        vpre[(half, jc)] = dma(
                            "gpsimd", wvb[jc][:, :],
                            wsrc0[128 * jc:128 * jc + 128, :], f"sW{jc}",
                            waits=[("sPE", vlast_pe[jc])], tag="v.wpre")
                    with contextlib.ExitStack() as s2:
                        lb = alloc_ln(s2)
                        emit_ln(lb,
                                lambda tc, toff=toff:
                                xb3[:, :, toff + 256 * tc:toff + 256 * tc + 256],
                                4, bigA, 0)
                        kb.barrier(f"b.kvln{half}")
                    with contextlib.ExitStack() as s2:
                        PEB = kb.val("sPE")
                        DVB = kb.val("sDV")
                        vprev_st = stg_prev
                        gall = [0]
                        for wsrc, njc, dscr_t, nheads in ((wvB, 2, v_scr, NKV),
                                                          (wlvB, 8, lv_scr, NH)):
                            for jc in range(njc):
                                if wsrc is wvB:
                                    wv = vpre[(half, jc)]
                                else:
                                    wv = dma("gpsimd", wvb[jc % 2][:, :],
                                             wsrc[128 * jc:128 * jc + 128, :],
                                             f"sW{jc % 2}",
                                             waits=[("sPE",
                                                     PEB + max(0, gall[0] - 8))],
                                             tag="v.w")
                                if wsrc is wlvB and jc >= 6:
                                    vlast_pe[jc % 2] = PEB + 8 * (jc + 1)
                                for tt in range(8):
                                    g = gall[0]
                                    gall[0] += 1
                                    pb = g % 2
                                    vM = None
                                    for c in range(DC):
                                        st, sp = (c == 0), (c == DC - 1)
                                        w = ([wv,
                                              ("sDV", DVB + max(0, g - 1))]
                                             if st else [])
                                        fn = (lambda e,
                                              l=bigA[:, 1024 * c + 128 * tt:
                                                     1024 * c + 128 * tt + 128],
                                              r=wvb[jc % 2][:, 512 * c:512 * c + 512],
                                              s1=st, s2=sp, ps=psV[pb][:, :]:
                                              e.matmul(ps, l, r, start=s1, stop=s2))
                                        if sp:
                                            vM = kb.op("tensor", fn, waits=w,
                                                       inc=("sPE", 1), tag="v.mm")
                                        else:
                                            kb.op("tensor", fn, waits=w, tag="v.mm")
                                    si = g % 4
                                    vw2 = [("sPE", vM)]
                                    if si in vprev_st:
                                        vw2.append(vprev_st[si])
                                    vE = kb.op(
                                        "vector",
                                        lambda e, ps=psV[pb], s=stg[si]:
                                        e.tensor_copy(s[:, 0:512], ps[:, :]),
                                        waits=vw2, inc=("sDV", 1), tag="v.ev")
                                    tg = 8 * half + tt
                                    for hh in range(4):
                                        hd0 = 4 * jc + hh
                                        vprev_st[si] = dma(
                                            "sync",
                                            dscr_t[:, (hd0 * NT + tg) * 128:
                                                   (hd0 * NT + tg) * 128 + 128],
                                            stg[si][:, 128 * hh:128 * hh + 128],
                                            f"sSt{si}", waits=[("sDV", vE)],
                                            tag="v.st")
                    with contextlib.ExitStack() as s2:
                        lqb = s2.enter_context(
                            nc.sbuf_tensor(nm(), [128, LC * 1024], BF16))
                        emit_proj_fm(wkB, DC, LC, bigA, 1024, 2,
                                     (wb, stg, t1, t2, psP), True, (cosFb, sinFb),
                                     toff, dst_dram=k_scr[:, toff:toff + 1024],
                                     tag="k")
                        emit_proj_fm(wlqB, DC, LC, bigA, 1024, 2,
                                     (wb, stg, t1, t2, psP), False, None, 0,
                                     dst_sbuf=lqb, tag="lq")
                        kb.barrier(f"b.lq{half}")
                        emit_proj_fm(wlkB, LC, DC, lqb, 1024, 2,
                                     (wb, stg, t1, t2, psP), True, (cosFb, sinFb),
                                     toff, dst_dram=lk_scr[:, toff:toff + 1024],
                                     tag="lk")
                        kb.barrier(f"b.kv{half}")
            if not enabled("attn"):
                kb.check(); kb.emit(blk); return nc, kb

        with contextlib.ExitStack() as s34:
            maskS = s34.enter_context(nc.sbuf_tensor("maskS", [128, 16 * 512], BF16))
            kbuf = [s34.enter_context(nc.sbuf_tensor(nm(), [128, S], BF16))
                    for _ in range(2)]
            vbuf = [s34.enter_context(nc.sbuf_tensor(nm(), [128, S], BF16))
                    for _ in range(2)]
            qbuf = [s34.enter_context(nc.sbuf_tensor(nm(), [128, NO], BF16))
                    for _ in range(2)]
            Pb = [s34.enter_context(nc.sbuf_tensor(nm(), [128, 512], BF16))
                  for _ in range(6)]
            PMb = [s34.enter_context(nc.sbuf_tensor(nm(), [128, 512], BF16))
                   for _ in range(6)]
            recS = [s34.enter_context(nc.sbuf_tensor(nm(), [128, 512], F32))
                    for _ in range(2)]
            psS3 = [s34.enter_context(nc.psum_tensor(nm(), [128, 512], F32))
                    for _ in range(4)]
            psPV = [s34.enter_context(nc.psum_tensor(nm(), [128, 512], F32))
                    for _ in range(2)]
            psDen = [s34.enter_context(nc.psum_tensor(nm(), [128, 512], F32))
                     for _ in range(2)]
            dma("sync", maskS[:, :], maskI, "cst", tag="at.m")
            kb.barrier("b.at0")

            att = {"gs": 0, "hc": 0}

            def attention_phase(phase):
                k_src = k_scr if phase == 0 else lk_scr
                v_src = v_scr if phase == 0 else lv_scr
                for h in range(32):
                    hb = h % 2
                    kv = (h // 4) if phase == 0 else h
                    hw = [("aC", max(0, att["gs"] - 24))]
                    vX = None
                    for dst, src in (
                            (kbuf[hb][:, :], k_src[128 * kv:128 * kv + 128, :]),
                            (vbuf[hb][:, :],
                             v_src[:, kv * NT * 128:(kv + 1) * NT * 128]),
                            (qbuf[hb][:, :], q_scr[128 * h:128 * h + 128, :])):
                        vX = dma("sync", dst, src, f"sX{hb}", waits=hw, tag="at.ld")
                    for cc in range(2):
                        K = SLOT_KT[cc]
                        att["hc"] += 1
                        hc = att["hc"]
                        dp = (hc - 1) % 2
                        plist = []

                        def emit_pvden(kt, K=K, hb=hb, dp=dp, hc=hc):
                            src_b, vsrc = plist[kt]
                            st, sp = (kt == 0), (kt == K - 1)
                            w = [vsrc]
                            if st:
                                w.append(("aE", max(0, hc - 2)))
                            kb.op("tensor",
                                  lambda e, s=src_b,
                                  v=vbuf[hb][:, 128 * kt:128 * kt + 128],
                                  s1=st, s2=sp, d=dp:
                                  e.matmul(psPV[d][:, :], v, s, start=s1, stop=s2),
                                  waits=w, tag="at.pv")
                            kb.op("tensor",
                                  lambda e, s=src_b, s1=st, s2=sp, d=dp:
                                  e.matmul(psDen[d][:, :], onesb[:, :], s,
                                           start=s1, stop=s2),
                                  inc=("aC", 1), tag="at.den")

                        for kt in range(K):
                            att["gs"] += 1
                            gs = att["gs"]
                            bank = (gs - 1) % 4
                            pbank = (gs - 1) % 6
                            vS = kb.op("tensor",
                                       lambda e, ps=psS3[bank][:, :],
                                       l=kbuf[hb][:, 128 * kt:128 * kt + 128],
                                       r=qbuf[hb][:, 512 * cc:512 * cc + 512]:
                                       e.matmul(ps, l, r, start=True, stop=True),
                                       waits=[vX, ("aP", max(0, gs - 4))],
                                       inc=("aS", 1), tag="at.s")
                            bcol = 8 * cc + kt
                            vP = kb.op("scalar",
                                       lambda e, ps=psS3[bank][:, :],
                                       o=Pb[pbank][:, :], bc=bcol:
                                       e.activation(out=o, in_=ps, func=AF.Exp,
                                                    scale=SCALE,
                                                    bias=biasT[:, bc:bc + 1]),
                                       waits=[("aS", vS), ("aC", max(0, gs - 6))],
                                       inc=("aP", 1), tag="at.e")
                            masked = (cc == 0) or (kt >= 8)
                            if masked:
                                vMk = kb.op("vector",
                                            lambda e, i=Pb[pbank][:, :],
                                            o=PMb[pbank][:, :],
                                            m=maskS[:, 512 * kt:512 * kt + 512]:
                                            e.tensor_mul(o, i, m),
                                            waits=[("aP", vP),
                                                   ("aC", max(0, gs - 6))],
                                            inc=("aM", 1), tag="at.mk")
                                plist.append((PMb[pbank][:, :], ("aM", vMk)))
                            else:
                                plist.append((Pb[pbank][:, :], ("aP", vP)))
                            if kt >= 2:
                                emit_pvden(kt - 2)
                        emit_pvden(K - 2)
                        emit_pvden(K - 1)
                        vCend = kb.val("aC")

                        def epi(e, d=dp, hh=h, ccc=cc, rb=recS[hc % 2]):
                            e.drain()
                            e.reciprocal(rb[:, :], psDen[d][:, :])
                            e.drain()
                            return e.tensor_mul(
                                bigA[:, 1024 * hh + 512 * ccc:
                                     1024 * hh + 512 * ccc + 512],
                                psPV[d][:, :], rb[:, :])
                        kb.op("vector", epi, waits=[("aC", vCend)],
                              inc=("aE", 1), tag="at.ep")

            def oproj_phase(wsrc, tagp):
                with contextlib.ExitStack() as sc:
                    wb2 = [sc.enter_context(
                        nc.sbuf_tensor(nm(), [128, DC * 128], BF16))
                        for _ in range(3)]
                    ost = [sc.enter_context(nc.sbuf_tensor(nm(), [128, 512], F32))
                           for _ in range(4)]
                    psO = [psS3[0], psS3[1]]
                    vWs = {}
                    for j in range(2):
                        vWs[j] = dma("gpsimd", wb2[j % 3][:, :],
                                     wsrc[128 * j:128 * j + 128, :],
                                     f"oW{j % 3}", tag=tagp + ".wpre")
                    kb.barrier("b." + tagp + "0")
                    PEB = kb.val("sPE")
                    DVB = kb.val("sDV")
                    aprev = {}
                    for j in range(DC):
                        if j + 2 < DC:
                            vWs[j + 2] = dma(
                                "gpsimd", wb2[(j + 2) % 3][:, :],
                                wsrc[128 * (j + 2):128 * (j + 2) + 128, :],
                                f"oW{(j + 2) % 3}",
                                waits=[("sPE", PEB + max(0, j * 2))],
                                tag=tagp + ".w")
                        wv = vWs[j]
                        for tc in range(2):
                            g = j * 2 + tc
                            pb = g % 2
                            vM = None
                            for c in range(DC):
                                st, sp = (c == 0), (c == DC - 1)
                                w = ([wv, ("sDV", DVB + max(0, g - 1))]
                                     if st else [])
                                fn = (lambda e, l=wb2[j % 3][:, 128 * c:128 * c + 128],
                                      r=bigA[:, 1024 * c + 512 * tc:
                                             1024 * c + 512 * tc + 512],
                                      s1=st, s2=sp, ps=psO[pb][:, :]:
                                      e.matmul(ps, l, r, start=s1, stop=s2))
                                if sp:
                                    vM = kb.op("tensor", fn, waits=w,
                                               inc=("sPE", 1), tag=tagp + ".mm")
                                else:
                                    kb.op("tensor", fn, waits=w, tag=tagp + ".mm")
                            si = g % 4
                            ow = [("sPE", vM)]
                            if si in aprev:
                                ow.append(aprev[si])
                            vE = kb.op("vector",
                                       lambda e, ps=psO[pb][:, :], s=ost[si][:, :]:
                                       e.tensor_copy(s, ps),
                                       waits=ow, inc=("sDV", 1), tag=tagp + ".ev")
                            aprev[si] = dma(
                                "gpsimd",
                                outT[128 * j:128 * j + 128, 512 * tc:512 * tc + 512],
                                ost[si][:, :], f"sAc{si}", waits=[("sDV", vE)],
                                tag=tagp + ".acc", accum=True)

            attention_phase(0)
            if expose:
                kb.barrier("b.a1")
                for i in range(8):
                    dma("sync", at_dbg[:, 4096 * i:4096 * i + 4096],
                        bigA[:, 4096 * i:4096 * i + 4096], "sSt0", tag="dbg.at")
                kb.barrier("b.dbga")
            if enabled("wo"):
                oproj_phase(woB, "wo")
                kb.barrier("b.wo")
            if enabled("lat"):
                attention_phase(1)
                if expose:
                    kb.barrier("b.a2")
                    for i in range(8):
                        dma("sync", lat_dbg[:, 4096 * i:4096 * i + 4096],
                            bigA[:, 4096 * i:4096 * i + 4096], "sSt0", tag="dbg.lat")
                    kb.barrier("b.dbgl")
            if enabled("wlo"):
                oproj_phase(wloB, "lo")
                kb.barrier("b.end")

        kb.check()
        kb.emit(blk)
    return nc, kb


def kernel(L0_attn_hidden_states, L0_attn_w_q, L0_attn_w_k, L0_attn_w_v, L0_attn_w_o,
           L0_attn_w_lq, L0_attn_w_lk, L0_attn_w_lv, L0_attn_w_lo,
           L0_ffn_base_W, L0_ffn_base_b, L0_ffn_router_W, L0_ffn_router_b,
           L0_ffn_A, L0_ffn_B, L0_ffn_expert_map):
    global LAST_RES
    x = np.asarray(L0_attn_hidden_states, dtype=np.float32)

    shared = {
        "wqB": _wbands(np.asarray(L0_attn_w_q, np.float32), DC, 32, 128),
        "wkB": _wbands(np.asarray(L0_attn_w_k, np.float32), DC, 8, 128),
        "wvB": _wbands(np.asarray(L0_attn_w_v, np.float32), DC, 2, 512),
        "wlqB": _wbands(np.asarray(L0_attn_w_lq, np.float32), DC, 8, 128),
        "wlkB": _wbands(np.asarray(L0_attn_w_lk, np.float32), LC, 32, 128),
        "wlvB": _wbands(np.asarray(L0_attn_w_lv, np.float32), DC, 8, 512),
        "woB": _wbands(np.asarray(L0_attn_w_o, np.float32), DC, 32, 128),
        "wloB": _wbands(np.asarray(L0_attn_w_lo, np.float32), DC, 32, 128),
        "bWB": _wbands(np.ascontiguousarray(np.asarray(L0_ffn_base_W, np.float32).T),
                       DC, 32, 128),
        "rWB": _wbands(np.ascontiguousarray(np.asarray(L0_ffn_router_W, np.float32).T),
                       DC, 1, NR),
        "ATB": _wbands(np.ascontiguousarray(
            np.asarray(L0_ffn_A, np.float32).reshape(128, D).T), DC, 1, 128),
        "BTB": np.ascontiguousarray(
            np.asarray(L0_ffn_B, np.float32).transpose(0, 2, 1).reshape(128, D)),
        "bbI": np.ascontiguousarray(
            np.asarray(L0_ffn_base_b, np.float32).reshape(32, 128).T),
        "rbI": np.asarray(L0_ffn_router_b, np.float32).reshape(NR, 1),
        "emapI": np.asarray(L0_ffn_expert_map, np.int32).reshape(NR, 1),
        "iotaI": np.tile(np.arange(NE, dtype=np.float32)[None, :], (NR, 1)),
        "p8I": _p8pad(),
        "onesI": np.ones((128, 128), dtype=BF),
        "eyeI": np.eye(128, dtype=np.float32),
    }
    percore = []
    for c in range(8):
        b, p = c // 2, c % 2
        own, cosO, sinO, cosF, sinF, biasA, masks = _host_consts(p)
        xt = np.ascontiguousarray(x[b].T)
        xo = np.ascontiguousarray(xt[:, own])
        m = dict(shared)
        m.update({
            "xoH": _xmajor(xo, NO),
            "xbH": _xmajor(xt, S),
            "cosO": cosO.astype(BF), "sinO": sinO.astype(BF),
            "cosF": np.ascontiguousarray(cosF).astype(BF),
            "sinF": np.ascontiguousarray(sinF).astype(BF),
            "biasI": biasA, "maskI": masks,
        })
        percore.append(m)

    nc, kb = build_program(UPTO, DEV_EXPOSE, SUB2A)
    res = run_bass_kernel_spmd(nc, percore, core_ids=list(range(8)))
    LAST_RES = res

    out = np.empty((B, S, D), dtype=np.float32)
    for c in range(8):
        b, p = c // 2, c % 2
        own = np.concatenate([np.arange(128 * t, 128 * t + 128)
                              for t in OWN_TILES[p]])
        out[b, own, :] = res.results[c]["outT"].T
    return out
